# version 07
# baseline (speedup 1.0000x reference)
import numpy as np
from collections import defaultdict
from contextlib import ExitStack

import concourse.bass as bass
import concourse.mybir as mybir
from concourse.bass_utils import run_bass_kernel_spmd

F32 = mybir.dt.float32
BF16 = mybir.dt.bfloat16
AF = mybir.ActivationFunctionType
ALU = mybir.AluOpType
AX = mybir.AxisListType

N_CORES = 8
B = 16
T = 2048
D = 1024
H = 16
DH = 64
NB = 2
NTOK = NB * T
CH = 128
NE = 8
FS = 2048
EPS = 1e-5

PHASE = "full"
NCHUNK = NTOK // CH
OP_LIMIT = None
SKIP = set()
DBG = False

ENGS = ("sync", "tensor", "vector", "scalar", "gpsimd")


class Em:

    def __init__(self):
        self.q = {e: [] for e in ENGS}
        self.cnt = defaultdict(int)
        self.seen = {e: defaultdict(int) for e in ENGS}

    def op(self, eng, fn, waits=(), inc=None, inck=1, tag=None):
        self.nops = getattr(self, "nops", 0) + 1
        if OP_LIMIT is not None and self.nops > OP_LIMIT:
            return
        if tag is not None and tag in SKIP:
            return
        wl = []
        for s, v in waits:
            if v > self.seen[eng][s]:
                self.seen[eng][s] = v
                wl.append((s, v))
        if inc is not None:
            self.cnt[inc] += inck

        def run(Eg, sems, wl=wl, fn=fn, inc=inc, inck=inck):
            for s, v in wl:
                Eg.wait_ge(sems[s], v)
            ins = fn(Eg)
            if inc is not None:
                ins.then_inc(sems[inc], inck)
        self.q[eng].append(run)

    def w(self, *names):
        return [(n, self.cnt[n]) for n in names]

    def barrier(self, mark_fns, extra_dma=()):
        marks = []
        pre = [(s, self.cnt[s]) for s in ("dve", "act", "pool")]
        for e, s in (("tensor", "pe"), ("vector", "dve"), ("scalar", "act"),
                     ("gpsimd", "pool")):
            self.op(e, mark_fns[e], waits=(pre if e == "tensor" else ()),
                    inc=s)
            marks.append((s, self.cnt[s]))
        waits = marks + [(n, self.cnt[n]) for n in extra_dma]
        for e in ENGS:
            self.op(e, mark_fns[e] if e in mark_fns else (lambda Eg: Eg.nop()),
                    waits=waits)


def build_nc():
    nc = bass.Bass("TRN2", target_bir_lowering=False)
    em = Em()

    x_h = nc.dram_tensor("x", [NTOK, D], F32, kind="ExternalInput").ap()
    wq_h = nc.dram_tensor("wqT", [D, D], F32, kind="ExternalInput").ap()
    wk_h = nc.dram_tensor("wkT", [D, D], F32, kind="ExternalInput").ap()
    wv_h = nc.dram_tensor("wvT", [D, D], F32, kind="ExternalInput").ap()
    wo_h = nc.dram_tensor("woT", [D, D], F32, kind="ExternalInput").ap()
    qgam_h = nc.dram_tensor("qgam", [128, D], F32, kind="ExternalInput").ap()
    qbet_h = nc.dram_tensor("qbet", [128, D], F32, kind="ExternalInput").ap()
    kgam_h = nc.dram_tensor("kgam", [128, D], F32, kind="ExternalInput").ap()
    kbet_h = nc.dram_tensor("kbet", [128, D], F32, kind="ExternalInput").ap()
    wr_h = nc.dram_tensor("wrT", [128, 8 * NE], F32, kind="ExternalInput").ap()
    w1_h = nc.dram_tensor("w1T", [NE, D, D], F32, kind="ExternalInput").ap()
    w2_h = nc.dram_tensor("w2T", [NE, D, D], F32, kind="ExternalInput").ap()
    ws1_h = nc.dram_tensor("ws1T", [D, FS], F32, kind="ExternalInput").ap()
    ws2_h = nc.dram_tensor("ws2T", [FS, D], F32, kind="ExternalInput").ap()
    out_h = nc.dram_tensor("out", [NTOK, D], F32, kind="ExternalOutput").ap()
    y_h = nc.dram_tensor("y_scr", [NTOK, D], F32, kind="Internal").ap()

    x_r = x_h.rearrange("(n p) c -> p n c", p=128)
    y_r = y_h.rearrange("(n p) c -> p n c", p=128)
    o_r = out_h.rearrange("(n p) c -> p n c", p=128)

    ctx = ExitStack()
    sems = {}
    for s in ("pe", "dve", "act", "pool", "sp",
              "dx", "dy", "dyl", "dw", "do"):
        sems[s] = ctx.enter_context(nc.semaphore(s))

    c_iota = ctx.enter_context(nc.sbuf_tensor("c_iota", [128, 128], F32))
    c_id = ctx.enter_context(nc.sbuf_tensor("c_id", [128, 128], BF16))
    c_mask = ctx.enter_context(nc.sbuf_tensor("c_mask", [128, 128], BF16))
    c_eps = ctx.enter_context(nc.sbuf_tensor("c_eps", [128, 1], F32))
    c_sq = ctx.enter_context(nc.sbuf_tensor("c_sq", [128, D], F32))
    c_t1 = ctx.enter_context(nc.sbuf_tensor("c_t1", [128, D], F32))
    c_ln = ctx.enter_context(nc.sbuf_tensor("c_ln", [128, D], F32))
    c_st = ctx.enter_context(nc.sbuf_tensor("c_st", [128, 128], F32))
    c_wr = ctx.enter_context(nc.sbuf_tensor("c_wr", [128, 8 * NE], BF16))
    p_all = ctx.enter_context(nc.psum_tensor("p_all", [128, 3072], F32))
    p_bf = ctx.enter_context(nc.psum_tensor("p_bf", [128, 2048], BF16))


    em.op("gpsimd", lambda Eg: Eg.iota(c_iota[:], pattern=[[1, 128]], base=0,
                                       channel_multiplier=-1,
                                       allow_small_or_imprecise_dtypes=True),
          inc="pool")
    em.op("vector", lambda Eg: Eg.tensor_scalar(
        c_id[:], c_iota[:], 0.0, None, ALU.is_equal), waits=em.w("pool"))
    em.op("vector", lambda Eg: Eg.tensor_scalar(
        c_mask[:], c_iota[:], 0.0, None, ALU.is_ge))
    em.op("vector", lambda Eg: Eg.memset(c_eps[:], EPS), inc="dve")
    em.op("gpsimd", lambda Eg: Eg.dma_start(
        out=c_wr[:], in_=wr_h), inc="dw", inck=16)

    actx = ExitStack()
    s_x = [actx.enter_context(nc.sbuf_tensor(f"s_x{i}", [128, 2, D], F32))
           for i in range(2)]
    s_y = [actx.enter_context(nc.sbuf_tensor(f"s_y{i}", [128, 2, D], F32))
           for i in range(2)]
    s_wq = actx.enter_context(nc.sbuf_tensor("s_wq", [128, 8, D], BF16))
    s_wk = actx.enter_context(nc.sbuf_tensor("s_wk", [128, 8, D], BF16))
    s_wv = actx.enter_context(nc.sbuf_tensor("s_wv", [128, 8, D], BF16))
    s_wo = actx.enter_context(nc.sbuf_tensor("s_wo", [128, 8, D], BF16))
    s_qgam = actx.enter_context(nc.sbuf_tensor("s_qgam", [128, D], BF16))
    s_qbet = actx.enter_context(nc.sbuf_tensor("s_qbet", [128, D], BF16))
    s_kgam = actx.enter_context(nc.sbuf_tensor("s_kgam", [128, D], BF16))
    s_kbet = actx.enter_context(nc.sbuf_tensor("s_kbet", [128, D], BF16))
    s_h1 = [actx.enter_context(nc.sbuf_tensor(f"s_h1{i}", [128, D], BF16))
            for i in range(2)]
    c_sq2 = actx.enter_context(nc.sbuf_tensor("c_sq2", [128, D], F32))
    s_h1T = actx.enter_context(nc.sbuf_tensor("s_h1T", [128, D], BF16))
    s_t2 = actx.enter_context(nc.sbuf_tensor("s_t2", [128, D], BF16))
    s_t3 = actx.enter_context(nc.sbuf_tensor("s_t3", [128, D], BF16))
    s_t2b = actx.enter_context(nc.sbuf_tensor("s_t2b", [128, D], BF16))
    s_t3b = actx.enter_context(nc.sbuf_tensor("s_t3b", [128, D], BF16))
    s_qh = actx.enter_context(nc.sbuf_tensor("s_qh", [128, D], BF16))
    s_kh = actx.enter_context(nc.sbuf_tensor("s_kh", [128, D], BF16))
    s_qT = actx.enter_context(nc.sbuf_tensor("s_qT", [64, 2 * D], BF16))
    s_kT = actx.enter_context(nc.sbuf_tensor("s_kT", [64, 2 * D], BF16))
    s_v = actx.enter_context(nc.sbuf_tensor("s_v", [128, D], BF16))
    s_at = actx.enter_context(nc.sbuf_tensor("s_at", [128, H * 128], BF16))
    s_o = actx.enter_context(nc.sbuf_tensor("s_o", [128, D], BF16))
    s_oT = actx.enter_context(nc.sbuf_tensor("s_oT", [128, D], BF16))
    s_S = actx.enter_context(nc.sbuf_tensor("s_S", [64, D], F32))
    s_Sb = actx.enter_context(nc.sbuf_tensor("s_Sb", [64, D], BF16))

    for w_hb, s_w in ((wq_h, s_wq), (wk_h, s_wk), (wv_h, s_wv), (wo_h, s_wo)):
        em.op("gpsimd", lambda Eg, w_hb=w_hb, s_w=s_w: Eg.dma_start(
            out=s_w[:], in_=w_hb.rearrange("(i p) f -> p i f", p=128)),
            inc="dw", inck=16)
    DW_QK = em.cnt["dw"] - 32
    DW_V = em.cnt["dw"] - 16
    DW_O = em.cnt["dw"]
    for g_hb, s_g0 in ((qgam_h, s_qgam), (qbet_h, s_qbet),
                       (kgam_h, s_kgam), (kbet_h, s_kbet)):
        em.op("gpsimd", lambda Eg, g_hb=g_hb, s_g0=s_g0: Eg.dma_start(
            out=s_g0[:], in_=g_hb), inc="dw", inck=16)
    DW_ATTN = em.cnt["dw"]

    def ln_smalls(n, b=0):
        cs = lambda j: c_st[:, b + j:b + j + 1]
        em.op("gpsimd", lambda Eg: Eg.tensor_scalar(
            cs(2), cs(0), -1.0 / n, None, ALU.mult),
            waits=em.w("act", "dve"))
        em.op("gpsimd", lambda Eg: Eg.tensor_tensor(
            cs(6), cs(2), cs(2), ALU.mult))
        em.op("gpsimd", lambda Eg: Eg.tensor_scalar(
            cs(3), cs(1), 1.0 / n, None, ALU.mult))
        em.op("gpsimd", lambda Eg: Eg.tensor_tensor(
            cs(3), cs(3), cs(6), ALU.subtract), inc="pool")
        em.op("scalar", lambda Eg: Eg.activation(
            cs(7), cs(3), AF.Ln, bias=c_eps[:]), waits=em.w("pool"))
        em.op("scalar", lambda Eg: Eg.activation(
            cs(4), cs(7), AF.Exp, scale=-0.5), inc="act")
        em.op("gpsimd", lambda Eg: Eg.tensor_tensor(
            cs(5), cs(2), cs(4), ALU.mult), waits=em.w("act"), inc="pool")

    st_y = 0
    tile_marks = {}
    n_tiles = NCHUNK // 2
    h1_mark = {}
    marks = {}

    def emit_xln(cc):
        tl2 = cc // 2
        xcc = s_x[tl2 % 2][:, cc % 2, :]
        W_X2 = [("dx", (tl2 + 1) * 16)]
        em.op("vector", lambda Eg, xcc=xcc: Eg.tensor_reduce(
            c_st[:, 104:105], xcc, AX.X, ALU.add),
            waits=W_X2 + em.w("act", "pool"))
        em.op("scalar", lambda Eg, xcc=xcc: Eg.activation(
            c_sq2[:], xcc, AF.Square, accum_out=c_st[:, 105:106]),
            waits=W_X2 + em.w("dve", "pool"), inc="act")
        ln_smalls(D, b=104)
        em.op("vector", lambda Eg, xcc=xcc, cc=cc: Eg.tensor_scalar(
            s_h1[cc % 2][:], xcc, c_st[:, 108:109], c_st[:, 109:110],
            ALU.mult, ALU.add),
            waits=em.w("pe", "act", "pool"), inc="dve")
        h1_mark[cc] = em.cnt["dve"]

    for c in range(NCHUNK):
        tl, m = c // 2, c % 2
        l = c % (T // CH)
        par = tl % 2
        xc = s_x[par][:, m, :]

        if m == 0:
            if tl == 0:
                em.op("sync", lambda Eg: Eg.dma_start(
                    out=s_x[0][:], in_=x_r[:, 0:2, :]),
                    waits=em.w("dve", "pool"), inc="dx", inck=16)
                if n_tiles > 1:
                    em.op("sync", lambda Eg: Eg.dma_start(
                        out=s_x[1][:], in_=x_r[:, 2:4, :]),
                        inc="dx", inck=16)
            elif tl + 1 < n_tiles:
                em.op("sync", lambda Eg, tl=tl: Eg.dma_start(
                    out=s_x[(tl + 1) % 2][:],
                    in_=x_r[:, (tl + 1) * 2:(tl + 1) * 2 + 2, :]),
                    waits=tile_marks.get(tl - 1, ()), inc="dx", inck=16)

        if c == 0:
            emit_xln(0)

        tw = [("dve", max(h1_mark[c], marks.get(("otev", c - 1), 0)))]
        for j in range(8):
            em.op("tensor", lambda Eg, j=j, c=c: Eg.transpose(
                p_bf[:, j * 128:(j + 1) * 128],
                s_h1[c % 2][:, j * 128:(j + 1) * 128],
                c_id[:]), waits=(tw if j == 0 else ()),
                inc=("pe" if j == 7 else None), tag="tp")
        em.op("vector", lambda Eg: Eg.tensor_copy(s_h1T[:], p_bf[:, 0:D]),
              waits=em.w("pe"), inc="dve", tag="tp")
        marks[("h1T", c)] = em.cnt["dve"]

        h1T = s_h1T[:].rearrange("p (i t) -> p i t", i=8)
        qk_w = [("dw", DW_QK),
                ("dve", max(marks[("h1T", c)],
                            marks.get(("atev", c - 1), 0),
                            marks.get(("sadd", c - 1), 0))),
                ("act", marks.get(("ksq", c - 1), 0))]
        for i in range(8):
            for s_w, base in ((s_wq, 0), (s_wk, 1024)):
                for hf in range(2):
                    em.op("tensor",
                          lambda Eg, i=i, s_w=s_w, base=base, hf=hf: Eg.matmul(
                              p_all[:, base + hf * 512: base + (hf + 1) * 512],
                              h1T[:, i, :], s_w[:, i, hf * 512:(hf + 1) * 512],
                              start=(i == 0), stop=(i == 7)),
                          waits=(qk_w if i == 0 else ()),
                          inc="pe" if (i == 7 and base == 1024 and hf == 1)
                          else None, tag="qkv")
        QK_PE = em.cnt["pe"]
        v_w = [("dw", DW_V),
               ("dve", marks.get(("pln", c - 1), 0)),
               ("act", marks.get(("psq", c - 1), 0))]
        for i in range(8):
            for hf in range(2):
                em.op("tensor", lambda Eg, i=i, hf=hf: Eg.matmul(
                    p_all[:, 2048 + hf * 512: 2048 + (hf + 1) * 512],
                    h1T[:, i, :], s_wv[:, i, hf * 512:(hf + 1) * 512],
                    start=(i == 0), stop=(i == 7)),
                    waits=(v_w if i == 0 else ()),
                    inc="pe" if (i == 7 and hf == 1) else None, tag="qkv")
        QKV_PE = em.cnt["pe"]

        em.op("scalar", lambda Eg: Eg.copy(s_v[:], p_all[:, 2048:3072]),
              waits=[("pe", QKV_PE)], inc="act")

        if c + 1 < NCHUNK:
            emit_xln(c + 1)

        em.op("vector", lambda Eg: Eg.tensor_reduce(
            c_st[:, 8:24].unsqueeze(2),
            p_all[:, 0:D].rearrange("p (h d) -> p h d", d=DH),
            AX.X, ALU.add),
            waits=[("pe", QK_PE)] + em.w("act"), inc="dve")
        em.op("scalar", lambda Eg: Eg.activation(
            c_sq[:], p_all[:, 0:D], AF.Square),
            waits=[("pe", QK_PE)] + em.w("dve"), inc="act")
        em.op("vector", lambda Eg: Eg.tensor_reduce(
            c_st[:, 56:72].unsqueeze(2),
            p_all[:, 1024:2048].rearrange("p (h d) -> p h d", d=DH),
            AX.X, ALU.add), waits=em.w("act"))
        em.op("vector", lambda Eg: Eg.tensor_reduce(
            c_st[:, 24:40].unsqueeze(2),
            c_sq[:].rearrange("p (h d) -> p h d", d=DH), AX.X, ALU.add),
            inc="dve")
        em.op("scalar", lambda Eg: Eg.activation(
            c_sq[:], p_all[:, 1024:2048], AF.Square),
            waits=em.w("dve"), inc="act")
        marks[("ksq", c)] = em.cnt["act"]
        em.op("vector", lambda Eg: Eg.tensor_reduce(
            c_st[:, 72:88].unsqueeze(2),
            c_sq[:].rearrange("p (h d) -> p h d", d=DH), AX.X, ALU.add),
            waits=em.w("act"), inc="dve")
        em.op("gpsimd", lambda Eg: Eg.tensor_scalar(
            c_st[:, 8:24], c_st[:, 8:24], -1.0 / DH, None, ALU.mult),
            waits=em.w("dve"))
        em.op("gpsimd", lambda Eg: Eg.tensor_scalar(
            c_st[:, 56:72], c_st[:, 56:72], -1.0 / DH, None, ALU.mult))
        em.op("gpsimd", lambda Eg: Eg.tensor_tensor(
            c_st[:, 40:56], c_st[:, 8:24], c_st[:, 8:24], ALU.mult))
        em.op("gpsimd", lambda Eg: Eg.tensor_tensor(
            c_st[:, 88:104], c_st[:, 56:72], c_st[:, 56:72], ALU.mult))
        em.op("gpsimd", lambda Eg: Eg.tensor_scalar(
            c_st[:, 24:40], c_st[:, 24:40], 1.0 / DH, None, ALU.mult))
        em.op("gpsimd", lambda Eg: Eg.tensor_scalar(
            c_st[:, 72:88], c_st[:, 72:88], 1.0 / DH, None, ALU.mult))
        em.op("gpsimd", lambda Eg: Eg.tensor_tensor(
            c_st[:, 24:40], c_st[:, 24:40], c_st[:, 40:56], ALU.subtract))
        em.op("gpsimd", lambda Eg: Eg.tensor_tensor(
            c_st[:, 72:88], c_st[:, 72:88], c_st[:, 88:104], ALU.subtract),
            inc="pool")
        em.op("scalar", lambda Eg: Eg.activation(
            c_st[:, 24:40], c_st[:, 24:40], AF.Ln, bias=c_eps[:]),
            waits=em.w("pool"))
        em.op("scalar", lambda Eg: Eg.activation(
            c_st[:, 24:40], c_st[:, 24:40], AF.Exp, scale=-0.5))
        em.op("scalar", lambda Eg: Eg.activation(
            c_st[:, 72:88], c_st[:, 72:88], AF.Ln, bias=c_eps[:]))
        em.op("scalar", lambda Eg: Eg.activation(
            c_st[:, 72:88], c_st[:, 72:88], AF.Exp, scale=-0.5), inc="act")
        em.op("vector", lambda Eg: Eg.tensor_tensor(
            c_t1[:].rearrange("p (h d) -> p h d", d=DH),
            p_all[:, 0:D].rearrange("p (h d) -> p h d", d=DH),
            c_st[:, 8:24].to_broadcast([128, H, DH]), ALU.add),
            waits=em.w("pool"))
        em.op("vector", lambda Eg: Eg.tensor_tensor(
            c_ln[:].rearrange("p (h d) -> p h d", d=DH),
            p_all[:, 1024:2048].rearrange("p (h d) -> p h d", d=DH),
            c_st[:, 56:72].to_broadcast([128, H, DH]), ALU.add))
        em.op("vector", lambda Eg: Eg.tensor_tensor(
            s_t2[:].rearrange("p (h d) -> p h d", d=DH),
            c_t1[:].rearrange("p (h d) -> p h d", d=DH),
            c_st[:, 24:40].to_broadcast([128, H, DH]), ALU.mult),
            waits=em.w("act", "pool"))
        em.op("vector", lambda Eg: Eg.tensor_tensor(
            s_t2b[:].rearrange("p (h d) -> p h d", d=DH),
            c_ln[:].rearrange("p (h d) -> p h d", d=DH),
            c_st[:, 72:88].to_broadcast([128, H, DH]), ALU.mult),
            inc="dve")
        em.op("gpsimd", lambda Eg: Eg.tensor_tensor(
            s_t3b[:], s_t2b[:], s_kgam[:], ALU.mult),
            waits=[("dw", DW_ATTN)] + em.w("dve"))
        em.op("vector", lambda Eg: Eg.tensor_tensor(
            s_t3[:], s_t2[:], s_qgam[:], ALU.mult),
            waits=[("dw", DW_ATTN)])
        em.op("vector", lambda Eg: Eg.tensor_tensor(
            s_qh[:], s_t3[:], s_qbet[:], ALU.add), inc="dve")
        QH_DVE = em.cnt["dve"]
        em.op("gpsimd", lambda Eg: Eg.tensor_tensor(
            s_kh[:], s_t3b[:], s_kbet[:], ALU.add), inc="pool")
        KH_POOL = em.cnt["pool"]

        for (hsrc, dstT, tw2) in ((s_qh, s_qT, None),
                                  (s_kh, s_kT, None)):
            tw2 = ([("dve", QH_DVE)] if hsrc is s_qh
                   else [("pool", KH_POOL), ("dve", em.cnt["dve"])])
            for h in range(H):
                em.op("tensor", lambda Eg, hsrc=hsrc, h=h: Eg.matmul(
                    p_all[0:64, h * 128:(h + 1) * 128],
                    hsrc[:, h * 64:(h + 1) * 64], c_id[:],
                    start=True, stop=True),
                    waits=(tw2 if h == 0 else ()),
                    inc="pe" if h == H - 1 else None, tag="tp")
            em.op("vector", lambda Eg, dstT=dstT: Eg.tensor_copy(
                dstT[:], p_all[0:64, 0:2048]),
                waits=em.w("pe"), inc="dve", tag="tp")
        QKT_DVE = em.cnt["dve"]

        for h in range(H):
            em.op("tensor", lambda Eg, h=h: Eg.matmul(
                p_all[:, h * 128:(h + 1) * 128],
                s_kT[0:64, h * 128:(h + 1) * 128],
                s_qT[0:64, h * 128:(h + 1) * 128],
                start=True, stop=True),
                waits=(([("dve", QKT_DVE)] + em.w("act")) if h == 0 else ()),
                inc="pe" if h == H - 1 else None, tag="at")
        AT_PE = em.cnt["pe"]
        em.op("vector", lambda Eg: Eg.tensor_tensor(
            s_at[:].rearrange("p (h t) -> p h t", h=H),
            p_all[:, 0:2048].rearrange("p (h t) -> p h t", h=H),
            c_mask[:].unsqueeze(1).to_broadcast([128, H, 128]), ALU.mult),
            waits=[("pe", AT_PE)], inc="dve", tag="red3")
        AT_DVE = em.cnt["dve"]
        marks[("atev", c)] = AT_DVE

        for h in range(H):
            last_o = (h == H - 1) and l == 0
            em.op("tensor", lambda Eg, h=h, l=l: Eg.matmul(
                p_all[:, 2048 + h * 64: 2048 + (h + 1) * 64],
                s_at[:, h * 128:(h + 1) * 128], s_v[:, h * 64:(h + 1) * 64],
                start=True, stop=(l == 0)),
                waits=(([("dve", AT_DVE)] + em.w("act")) if h == 0 else ()),
                inc="pe" if last_o else None, tag="at")
            if l > 0:
                em.op("tensor", lambda Eg, h=h: Eg.matmul(
                    p_all[:, 2048 + h * 64: 2048 + (h + 1) * 64],
                    s_qT[0:64, h * 128:(h + 1) * 128],
                    s_Sb[0:64, h * 64:(h + 1) * 64],
                    start=False, stop=True),
                    waits=(em.w("pool") if h == 0 else ()),
                    inc="pe" if h == H - 1 else None, tag="sup")
        O_PE = em.cnt["pe"]

        for h in range(H):
            em.op("tensor", lambda Eg, h=h: Eg.matmul(
                p_all[0:64, h * 64:(h + 1) * 64],
                s_kh[:, h * 64:(h + 1) * 64], s_v[:, h * 64:(h + 1) * 64],
                start=True, stop=True),
                waits=(([("dve", AT_DVE)] + em.w("act")) if h == 0 else ()),
                inc="pe" if h == H - 1 else None, tag="sup")
        SD_PE = em.cnt["pe"]
        if l == 0:
            em.op("vector", lambda Eg: Eg.tensor_copy(
                s_S[:], p_all[0:64, 0:D]),
                waits=[("pe", SD_PE), ("pe", O_PE)], inc="dve", tag="sup")
        else:
            em.op("vector", lambda Eg: Eg.tensor_tensor(
                s_S[:], s_S[:], p_all[0:64, 0:D], ALU.add),
                waits=[("pe", SD_PE), ("pe", O_PE)], inc="dve", tag="sup")
        marks[("sadd", c)] = em.cnt["dve"]
        em.op("gpsimd", lambda Eg: Eg.tensor_copy(s_Sb[:], s_S[:]),
              waits=em.w("dve"), inc="pool", tag="sup")

        em.op("vector", lambda Eg: Eg.tensor_copy(s_o[:], p_all[:, 2048:3072]),
              waits=[("pe", O_PE)], inc="dve")
        for j in range(8):
            em.op("tensor", lambda Eg, j=j: Eg.transpose(
                p_bf[:, j * 128:(j + 1) * 128], s_o[:, j * 128:(j + 1) * 128],
                c_id[:]), waits=(em.w("dve") if j == 0 else ()),
                inc="pe" if j == 7 else None, tag="tp")
        em.op("vector", lambda Eg: Eg.tensor_copy(s_oT[:], p_bf[:, 0:D]),
              waits=em.w("pe"), inc="dve", tag="tp")
        OT_DVE = em.cnt["dve"]
        marks[("otev", c)] = OT_DVE

        oT = s_oT[:].rearrange("p (i t) -> p i t", i=8)
        for i in range(8):
            for hf in range(2):
                em.op("tensor", lambda Eg, i=i, hf=hf: Eg.matmul(
                    p_all[:, 2048 + hf * 512: 2048 + (hf + 1) * 512],
                    oT[:, i, :], s_wo[:, i, hf * 512:(hf + 1) * 512],
                    start=(i == 0), stop=(i == 7)),
                    waits=(([("dw", DW_O), ("dve", OT_DVE)] + em.w("act"))
                           if i == 0 else ()),
                    inc="pe" if (i == 7 and hf == 1) else None, tag="ao")
        AO_PE = em.cnt["pe"]

        em.op("vector", lambda Eg: Eg.tensor_reduce(
            c_st[:, 0:1], p_all[:, 2048:3072], AX.X, ALU.add),
            waits=[("pe", AO_PE)] + em.w("act"), inc="dve", tag="ao")
        em.op("scalar", lambda Eg: Eg.activation(
            c_sq[:], p_all[:, 2048:3072], AF.Square, accum_out=c_st[:, 1:2]),
            waits=[("pe", AO_PE)] + em.w("dve"), inc="act", tag="actp")
        marks[("psq", c)] = em.cnt["act"]
        ln_smalls(D)
        em.op("vector", lambda Eg: Eg.tensor_scalar(
            c_ln[:], p_all[:, 2048:3072], c_st[:, 4:5], c_st[:, 5:6],
            ALU.mult, ALU.add), waits=em.w("act", "pool"),
            inc="dve", tag="ao")
        marks[("pln", c)] = em.cnt["dve"]
        em.op("gpsimd", lambda Eg, xc=xc, par=par, m=m: Eg.tensor_tensor(
            s_y[par][:, m, :], xc, c_ln[:], ALU.add),
            waits=em.w("dve") + [("dy", st_y * 16)], inc="pool")

        if DBG and c == 0:
            dumps = [(s_h1, 4), (s_qh, 6), (s_kh, 8), (s_v, 10),
                     (s_o, 12), (s_oT, 14), (s_h1T, 16), (c_st, 18),
                     (c_ln, 20), (c_sq, 22)]
            for ten, rb in dumps:
                em.op("gpsimd", lambda Eg, ten=ten, rb=rb: Eg.dma_start(
                    out=o_r[:, rb, 0:ten.shape[-1]], in_=ten[:]
                    if len(ten.shape) == 2 else ten[:].flatten_outer_dims()),
                    waits=em.w("dve", "act", "pool", "pe"), inc="do", inck=16)
            em.op("gpsimd", lambda Eg: Eg.dma_start(
                out=o_r[0:64, 24:26, :],
                in_=s_qT[:].rearrange("p (a b) -> p a b", a=2)),
                waits=em.w("dve", "act", "pool", "pe"), inc="do", inck=16)
            em.op("gpsimd", lambda Eg: Eg.dma_start(
                out=o_r[0:64, 26:28, :],
                in_=s_kT[:].rearrange("p (a b) -> p a b", a=2)),
                inc="do", inck=16)
            em.op("gpsimd", lambda Eg: Eg.dma_start(
                out=o_r[:, 28:30, :],
                in_=s_at[:].rearrange("p (a b) -> p a b", a=2)),
                inc="do", inck=16)
        if m == 1:
            em.op("sync", lambda Eg, tl=tl, par=par: Eg.dma_start(
                out=y_r[:, tl * 2:tl * 2 + 2, :], in_=s_y[par][:]),
                waits=em.w("pool"), inc="dy", inck=16)
            st_y += 1
            tile_marks[tl] = em.w("dve", "act", "pool")
            if PHASE == "attn_only":
                em.op("sync", lambda Eg, tl=tl, par=par: Eg.dma_start(
                    out=o_r[:, tl * 2:tl * 2 + 2, :], in_=s_y[par][:]),
                    inc="do", inck=16)

    DY_ALL = em.cnt["dy"]

    if PHASE == "full":
        mark_fns = {
            "tensor": lambda Eg: Eg.matmul(
                p_all[:, 0:128], c_id[:], c_id[:], start=True, stop=True),
            "vector": lambda Eg: Eg.tensor_copy(c_st[:, 126:127], c_eps[:]),
            "scalar": lambda Eg: Eg.copy(c_st[:, 125:126], c_eps[:]),
            "gpsimd": lambda Eg: Eg.tensor_copy(c_st[:, 124:125], c_eps[:]),
        }
        em.barrier(mark_fns, extra_dma=("dx", "dy"))
        actx.close()

        fctx = ExitStack()
        s_sl = [fctx.enter_context(
            nc.sbuf_tensor(f"s_sl{i}", [128, 8, D], BF16)) for i in range(4)]
        s_h2T = fctx.enter_context(
            nc.sbuf_tensor("s_h2T", [128, 8, 2048], BF16))
        s_acc = fctx.enter_context(nc.sbuf_tensor("s_acc", [128, 16, D], BF16))
        s_is = fctx.enter_context(nc.sbuf_tensor("s_is", [128, 16, 512], BF16))
        s_fy = fctx.enter_context(nc.sbuf_tensor("s_fy", [128, 4, D], F32))
        s_h2 = fctx.enter_context(nc.sbuf_tensor("s_h2", [128, 4, D], BF16))
        s_rl = fctx.enter_context(nc.sbuf_tensor("s_rl", [128, 2048], BF16))
        s_tg = fctx.enter_context(nc.sbuf_tensor("s_tg", [128, 2, D], BF16))
        s_fo = fctx.enter_context(nc.sbuf_tensor("s_fo", [128, 4, D], F32))
        s_lg = fctx.enter_context(nc.sbuf_tensor("s_lg", [128, 16, NE], F32))
        s_g = fctx.enter_context(nc.sbuf_tensor("s_g", [128, 16, NE], F32))
        s_gs = fctx.enter_context(nc.sbuf_tensor("s_gs", [128, 96], F32))

        NGRP = 2
        wld = [0]

        for g in range(NGRP):
            unit_srcs = [
                ws1_h[:, 0:1024].rearrange("(i p) f -> p i f", p=128),
                ws1_h[:, 1024:2048].rearrange("(i p) f -> p i f", p=128),
                ws2_h[0:1024, :].rearrange("(i p) f -> p i f", p=128),
                ws2_h[1024:2048, :].rearrange("(i p) f -> p i f", p=128),
            ]
            for e in range(NE):
                unit_srcs.append(w1_h[e].rearrange("(i p) f -> p i f", p=128))
                unit_srcs.append(w2_h[e].rearrange("(i p) f -> p i f", p=128))
            unit_wait = {}
            loaded = [-1]
            ev_mark = {}
            ev_mark2 = {}

            def issue_unit(u, unit_srcs=unit_srcs, unit_wait=unit_wait,
                           loaded=loaded):
                slot = s_sl[u % 4]
                wld[0] += 1
                em.op("gpsimd",
                      lambda Eg, slot=slot, src=unit_srcs[u]: Eg.dma_start(
                          out=slot[:], in_=src),
                      waits=em.w("pe"), inc="dw", inck=16)
                unit_wait[u] = em.cnt["dw"]
                loaded[0] = u

            for u in range(4):
                issue_unit(u)

            for tt in range(4):
                gt0 = g * 16 + tt * 4
                em.op("sync", lambda Eg, gt0=gt0: Eg.dma_start(
                    out=s_fy[:], in_=y_r[:, gt0:gt0 + 4, :]),
                    waits=[("dy", DY_ALL)] + em.w("dve", "act", "pool"),
                    inc="dyl", inck=16)
                DYL = em.cnt["dyl"]
                for n in range(4):
                    em.op("vector", lambda Eg, n=n: Eg.tensor_reduce(
                        c_st[:, 0:1], s_fy[:, n, :], AX.X, ALU.add),
                        waits=[("dyl", DYL)] + em.w("act"))
                    em.op("scalar", lambda Eg, n=n: Eg.activation(
                        c_sq[:], s_fy[:, n, :], AF.Square,
                        accum_out=c_st[:, 1:2]),
                        waits=[("dyl", DYL)] + em.w("dve"), inc="act")
                    ln_smalls(D)
                    em.op("vector", lambda Eg, n=n: Eg.tensor_scalar(
                        s_h2[:, n, :], s_fy[:, n, :], c_st[:, 4:5],
                        c_st[:, 5:6], ALU.mult, ALU.add),
                        waits=em.w("pe", "act", "pool"), inc="dve")
                    for j in range(8):
                        em.op("tensor", lambda Eg, n=n, j=j: Eg.transpose(
                            p_bf[:, j * 128:(j + 1) * 128],
                            s_h2[:, n, j * 128:(j + 1) * 128], c_id[:]),
                            waits=(em.w("dve") if j == 0 else ()),
                            inc="pe" if j == 7 else None)
                    tc0 = (tt * 4 + n) * 128
                    em.op("vector", lambda Eg, tc0=tc0: Eg.tensor_copy(
                        s_h2T[:, :, tc0:tc0 + 128],
                        p_bf[:, 0:D].rearrange("p (i t) -> p i t", i=8)),
                        waits=em.w("pe"), inc="dve")
                    H2T_DVE = em.cnt["dve"]
                    for i in range(8):
                        em.op("tensor", lambda Eg, tc0=tc0, i=i: Eg.matmul(
                            p_all[:, 0:NE], s_h2T[:, i, tc0:tc0 + 128],
                            c_wr[:, i * NE:(i + 1) * NE],
                            start=(i == 0), stop=(i == 7)),
                            waits=([("dve", H2T_DVE), ("dw", 16)]
                                   if i == 0 else ()),
                            inc="pe" if i == 7 else None)
                    em.op("vector", lambda Eg, tt=tt, n=n: Eg.tensor_copy(
                        s_lg[:, tt * 4 + n, :], p_all[:, 0:NE]),
                        waits=em.w("pe"), inc="dve")
                ev1 = {}
                rl1 = {}
                for g8 in range(8):
                    u = g8 // 4
                    reg = (g8 % 2) * 1024
                    for jj in range(2):
                        for i in range(8):
                            em.op("tensor",
                                  lambda Eg, u=u, g8=g8, reg=reg, jj=jj, i=i,
                                  tt=tt: Eg.matmul(
                                      p_all[:, reg + jj * 512:
                                            reg + jj * 512 + 512],
                                      s_sl[u][:, i, (g8 % 4) * 256 + jj * 128:
                                              (g8 % 4) * 256 + jj * 128 + 128],
                                      s_h2T[:, i, tt * 512:(tt + 1) * 512],
                                      start=(i == 0), stop=(i == 7)),
                                  waits=([("dw", unit_wait[u]),
                                          ("dve", ev1.get(g8 % 2,
                                                          em.cnt["dve"]
                                                          if g8 < 2 else 0)),
                                          ("act", rl1.get(g8 % 2, 0))]
                                         if (jj == 0 and i == 0) else ()),
                                  inc="pe" if (jj == 1 and i == 7) else None)
                    PE_Q = em.cnt["pe"]
                    em.op("scalar", lambda Eg, reg=reg, g8=g8: Eg.activation(
                        s_rl[:, reg:reg + 1024], p_all[:, reg:reg + 1024],
                        AF.Relu), waits=[("pe", PE_Q)], inc="act")
                    rl1[g8 % 2] = em.cnt["act"]
                    em.op("vector", lambda Eg, g8=g8, reg=reg: Eg.tensor_tensor(
                        s_is[:, 2 * g8:2 * g8 + 2, :],
                        p_all[:, reg:reg + 1024].rearrange(
                            "p (a b) -> p a b", a=2),
                        s_rl[:, reg:reg + 1024].rearrange(
                            "p (a b) -> p a b", a=2), ALU.mult),
                        waits=em.w("act"), inc="dve")
                    ev1[g8 % 2] = em.cnt["dve"]
                IS_DVE = em.cnt["dve"]
                for n in range(4):
                    for hf in range(2):
                        for i in range(16):
                            u = 2 + i // 8
                            em.op("tensor",
                                  lambda Eg, i=i, u=u, hf=hf, n=n: Eg.matmul(
                                      p_all[:, 2048 + hf * 512:
                                            2048 + (hf + 1) * 512],
                                      s_is[:, i, n * 128:(n + 1) * 128],
                                      s_sl[u][:, i % 8,
                                              hf * 512:(hf + 1) * 512],
                                      start=(i == 0), stop=(i == 15)),
                                  waits=(([("dve", IS_DVE),
                                           ("dw", unit_wait[2]),
                                           ("dw", unit_wait[3]),
                                           ("dve", ev_mark.get(hf, 0))]
                                          ) if i == 0 else ()),
                                  inc="pe" if i == 15 else None)
                        em.op("vector",
                              lambda Eg, tt=tt, n=n, hf=hf: Eg.tensor_copy(
                                  s_acc[:, tt * 4 + n,
                                        hf * 512:(hf + 1) * 512],
                                  p_all[:, 2048 + hf * 512:
                                        2048 + (hf + 1) * 512]),
                              waits=em.w("pe"), inc="dve")
                        ev_mark[hf] = em.cnt["dve"]

            m1 = s_gs[:, 0:16]
            m2 = s_gs[:, 16:32]
            w1c = s_gs[:, 32:48]
            w2c = s_gs[:, 48:64]
            ee = s_gs[:, 64:80]
            m1b = s_gs[:, 80:96]
            lg3 = s_lg[:]
            mk3 = c_sq[:, 0:128].rearrange("p (a b) -> p a b", a=16)
            e23 = c_ln[:, 0:128].rearrange("p (a b) -> p a b", a=16)
            em.op("vector", lambda Eg: Eg.tensor_reduce(
                m1.unsqueeze(2), lg3, AX.X, ALU.max),
                waits=em.w("act", "pool"), inc="dve")
            em.op("gpsimd", lambda Eg: Eg.tensor_copy(m1b, m1),
                  waits=em.w("dve"), inc="pool")
            em.op("vector", lambda Eg: Eg.tensor_tensor(
                s_g[:], lg3, m1b.to_broadcast([128, 16, NE]), ALU.is_equal),
                waits=em.w("pool"), inc="dve")
            em.op("gpsimd", lambda Eg: Eg.tensor_scalar(
                mk3, s_g[:], -1e9, None, ALU.mult), waits=em.w("dve"))
            em.op("gpsimd", lambda Eg: Eg.tensor_tensor(
                mk3, mk3, lg3, ALU.add), inc="pool")
            em.op("vector", lambda Eg: Eg.tensor_reduce(
                m2.unsqueeze(2), mk3, AX.X, ALU.max),
                waits=em.w("pool"), inc="dve")
            em.op("gpsimd", lambda Eg: Eg.tensor_tensor(
                ee, m2, m1b, ALU.subtract), waits=em.w("dve"))
            em.op("gpsimd", lambda Eg: Eg.tensor_copy(m1b, m2),
                  inc="pool")
            em.op("scalar", lambda Eg: Eg.activation(
                ee, ee, AF.Exp), waits=em.w("pool"), inc="act")
            em.op("gpsimd", lambda Eg: Eg.tensor_scalar(
                w1c, ee, 1.0, None, ALU.add), waits=em.w("act"), inc="pool")
            em.op("vector", lambda Eg: Eg.reciprocal(w1c, w1c),
                  waits=em.w("pool"), inc="dve")
            em.op("vector", lambda Eg: Eg.tensor_tensor(
                e23, mk3, m1b.to_broadcast([128, 16, NE]), ALU.is_equal),
                inc="dve")
            em.op("gpsimd", lambda Eg: Eg.tensor_scalar(
                w2c, w1c, -1.0, None, ALU.mult), waits=em.w("dve"))
            em.op("gpsimd", lambda Eg: Eg.tensor_scalar(
                w2c, w2c, 1.0, None, ALU.add))
            em.op("gpsimd", lambda Eg: Eg.tensor_tensor(
                s_g[:], s_g[:], w1c.to_broadcast([128, 16, NE]), ALU.mult))
            em.op("gpsimd", lambda Eg: Eg.tensor_tensor(
                e23, e23, w2c.to_broadcast([128, 16, NE]), ALU.mult))
            em.op("gpsimd", lambda Eg: Eg.tensor_tensor(
                s_g[:], s_g[:], e23, ALU.add), inc="pool")

            def emit_F(tt, g=g):
                gt0 = g * 16 + tt * 4
                em.op("sync", lambda Eg, gt0=gt0: Eg.dma_start(
                    out=s_fy[:], in_=y_r[:, gt0:gt0 + 4, :]),
                    waits=em.w("dve", "act", "pool"), inc="dyl", inck=16)
                DYL = em.cnt["dyl"]
                for n in range(4):
                    em.op("vector", lambda Eg, tt=tt, n=n: Eg.tensor_reduce(
                        c_st[:, 0:1], s_acc[:, tt * 4 + n, :], AX.X, ALU.add),
                        waits=em.w("act", "pool"))
                    em.op("scalar", lambda Eg, tt=tt, n=n: Eg.activation(
                        c_sq[:], s_acc[:, tt * 4 + n, :], AF.Square,
                        accum_out=c_st[:, 1:2]),
                        waits=em.w("dve", "pool"), inc="act")
                    ln_smalls(D)
                    em.op("vector", lambda Eg, tt=tt, n=n: Eg.tensor_scalar(
                        c_ln[:], s_acc[:, tt * 4 + n, :], c_st[:, 4:5],
                        c_st[:, 5:6], ALU.mult, ALU.add),
                        waits=em.w("act", "pool"), inc="dve")
                    em.op("gpsimd", lambda Eg, n=n: Eg.tensor_tensor(
                        s_fo[:, n, :], s_fy[:, n, :], c_ln[:], ALU.add),
                        waits=em.w("dve") + [("dyl", DYL),
                                             ("do", em.cnt["do"])],
                        inc="pool")
                em.op("sync", lambda Eg, gt0=gt0: Eg.dma_start(
                    out=o_r[:, gt0:gt0 + 4, :], in_=s_fo[:]),
                    waits=em.w("pool"), inc="do", inck=16)

            for e in range(NE):
                u1, u2 = 4 + 2 * e, 5 + 2 * e
                while loaded[0] < min(u2 + 2, len(unit_srcs) - 1):
                    issue_unit(loaded[0] + 1)
                sl1, sl2 = s_sl[u1 % 4], s_sl[u2 % 4]
                for tt in range(4):
                    ev1 = {}
                    rl1 = {}
                    for eg in range(4):
                        reg = (eg % 2) * 1024
                        for jj in range(2):
                            for i in range(8):
                                em.op("tensor",
                                      lambda Eg, sl1=sl1, eg=eg, reg=reg,
                                      jj=jj, i=i, tt=tt: Eg.matmul(
                                          p_all[:, reg + jj * 512:
                                                reg + jj * 512 + 512],
                                          sl1[:, i, eg * 256 + jj * 128:
                                              eg * 256 + jj * 128 + 128],
                                          s_h2T[:, i,
                                                tt * 512:(tt + 1) * 512],
                                          start=(i == 0), stop=(i == 7)),
                                      waits=([("dw", unit_wait[u1]),
                                              ("dve", ev1.get(eg % 2,
                                                              em.cnt["dve"]
                                                              if eg < 2
                                                              else 0)),
                                              ("act", rl1.get(eg % 2, 0))]
                                             if (jj == 0 and i == 0) else ()),
                                      inc="pe" if (jj == 1 and i == 7)
                                      else None)
                        PE_Q = em.cnt["pe"]
                        em.op("scalar", lambda Eg, reg=reg: Eg.activation(
                            s_rl[:, reg:reg + 1024], p_all[:, reg:reg + 1024],
                            AF.Relu), waits=[("pe", PE_Q)], inc="act")
                        rl1[eg % 2] = em.cnt["act"]
                        em.op("vector", lambda Eg, eg=eg, reg=reg:
                              Eg.tensor_tensor(
                                  s_is[:, 2 * eg:2 * eg + 2, :],
                                  p_all[:, reg:reg + 1024].rearrange(
                                      "p (a b) -> p a b", a=2),
                                  s_rl[:, reg:reg + 1024].rearrange(
                                      "p (a b) -> p a b", a=2), ALU.mult),
                              waits=em.w("act"), inc="dve")
                        ev1[eg % 2] = em.cnt["dve"]
                    IE_DVE = em.cnt["dve"]
                    for n in range(4):
                        for hf in range(2):
                            for i in range(8):
                                em.op("tensor",
                                      lambda Eg, sl2=sl2, i=i, hf=hf, n=n:
                                      Eg.matmul(
                                          p_all[:, 2048 + hf * 512:
                                                2048 + (hf + 1) * 512],
                                          s_is[:, i, n * 128:(n + 1) * 128],
                                          sl2[:, i, hf * 512:(hf + 1) * 512],
                                          start=(i == 0), stop=(i == 7)),
                                      waits=(([("dve", IE_DVE),
                                               ("dw", unit_wait[u2]),
                                               ("act", ev_mark2.get(hf, 0))]
                                              ) if i == 0 else ()),
                                      inc="pe" if i == 7 else None)
                            PE_O = em.cnt["pe"]
                            em.op("scalar", lambda Eg, tt=tt, n=n, e=e, hf=hf:
                                  Eg.activation(
                                      s_tg[:, n % 2, hf * 512:(hf + 1) * 512],
                                      p_all[:, 2048 + hf * 512:
                                            2048 + (hf + 1) * 512],
                                      AF.Copy,
                                      scale=s_g[:, tt * 4 + n, e:e + 1]),
                                  waits=[("pe", PE_O)] + em.w("dve", "pool"),
                                  inc="act")
                            ev_mark2[hf] = em.cnt["act"]
                        eng = "vector" if n % 2 == 0 else "gpsimd"
                        em.op(eng, lambda Eg, tt=tt, n=n: Eg.tensor_tensor(
                            s_acc[:, tt * 4 + n, :], s_acc[:, tt * 4 + n, :],
                            s_tg[:, n % 2, :], ALU.add),
                            waits=em.w("act"),
                            inc="dve" if eng == "vector" else "pool")
                    if e == NE - 1:
                        emit_F(tt)


    em.op("sync", lambda Eg: Eg.nop(), waits=[("do", em.cnt["do"])])

    blk = ctx.enter_context(nc.Block())
    for e in ENGS:
        def runner(Eobj, e=e):
            for fn in em.q[e]:
                fn(Eobj, sems)
        getattr(blk, e)(runner)
    if PHASE == "full":
        fctx.close()
    else:
        actx.close()
    ctx.close()
    return nc


_NC_CACHE = {}


def _get_nc():
    if PHASE not in _NC_CACHE:
        _NC_CACHE[PHASE] = build_nc()
    return _NC_CACHE[PHASE]


def kernel(L0_attn_x, L0_attn_Wq, L0_attn_Wk, L0_attn_Wv, L0_attn_Wo,
           L0_attn_q_gamma, L0_attn_q_beta, L0_attn_k_gamma, L0_attn_k_beta,
           L0_ffn_w_router, L0_ffn_w1_stack, L0_ffn_w2_stack,
           L0_ffn_ws1, L0_ffn_ws2):
    ca = np.ascontiguousarray
    x = ca(np.asarray(L0_attn_x, dtype=np.float32))
    wq = ca(np.asarray(L0_attn_Wq, np.float32).T)
    wk = ca(np.asarray(L0_attn_Wk, np.float32).T)
    wv = ca(np.asarray(L0_attn_Wv, np.float32).T)
    wo = ca(np.asarray(L0_attn_Wo, np.float32).T)
    qg = np.asarray(L0_attn_q_gamma, np.float32)
    qb = np.asarray(L0_attn_q_beta, np.float32)
    kg = np.asarray(L0_attn_k_gamma, np.float32)
    kb = np.asarray(L0_attn_k_beta, np.float32)
    rep = lambda v: ca(np.broadcast_to(np.tile(v, H)[None, :], (128, D)))
    wr = np.asarray(L0_ffn_w_router, np.float32).T
    wr = ca(wr.reshape(8, 128, NE).transpose(1, 0, 2).reshape(128, 8 * NE))
    w1 = ca(np.asarray(L0_ffn_w1_stack, np.float32).transpose(0, 2, 1))
    w2 = ca(np.asarray(L0_ffn_w2_stack, np.float32).transpose(0, 2, 1))
    ws1 = ca(np.asarray(L0_ffn_ws1, np.float32).T)
    ws2 = ca(np.asarray(L0_ffn_ws2, np.float32).T)

    shared = dict(wqT=wq, wkT=wk, wvT=wv, woT=wo,
                  qgam=rep(qg), qbet=rep(qb), kgam=rep(kg), kbet=rep(kb),
                  wrT=wr, w1T=w1, w2T=w2, ws1T=ws1, ws2T=ws2)
    in_maps = []
    for cid in range(N_CORES):
        m = dict(shared)
        m["x"] = ca(x[cid * NB:(cid + 1) * NB].reshape(NTOK, D))
        in_maps.append(m)

    nc = _get_nc()
    res = run_bass_kernel_spmd(nc, in_maps, core_ids=list(range(N_CORES)))
    outs = [np.asarray(r["out"], np.float32).reshape(NB, T, D)
            for r in res.results]
    return np.concatenate(outs, axis=0)


# version 08
# speedup vs baseline: 1.0249x; 1.0249x over previous
import numpy as np
from collections import defaultdict
from contextlib import ExitStack

import concourse.bass as bass
import concourse.mybir as mybir
from concourse.bass_utils import run_bass_kernel_spmd

F32 = mybir.dt.float32
BF16 = mybir.dt.bfloat16
AF = mybir.ActivationFunctionType
ALU = mybir.AluOpType
AX = mybir.AxisListType

N_CORES = 8
B = 16
T = 2048
D = 1024
H = 16
DH = 64
NB = 2
NTOK = NB * T
CH = 128
NE = 8
FS = 2048
EPS = 1e-5

PHASE = "full"
NCHUNK = NTOK // CH
OP_LIMIT = None
SKIP = set()
DBG = False

ENGS = ("sync", "tensor", "vector", "scalar", "gpsimd")


class Em:

    def __init__(self):
        self.q = {e: [] for e in ENGS}
        self.cnt = defaultdict(int)
        self.seen = {e: defaultdict(int) for e in ENGS}

    def op(self, eng, fn, waits=(), inc=None, inck=1, tag=None):
        self.nops = getattr(self, "nops", 0) + 1
        if OP_LIMIT is not None and self.nops > OP_LIMIT:
            return
        if tag is not None and tag in SKIP:
            return
        wl = []
        for s, v in waits:
            if v > self.seen[eng][s]:
                self.seen[eng][s] = v
                wl.append((s, v))
        if inc is not None:
            self.cnt[inc] += inck

        def run(Eg, sems, wl=wl, fn=fn, inc=inc, inck=inck):
            for s, v in wl:
                Eg.wait_ge(sems[s], v)
            ins = fn(Eg)
            if inc is not None:
                ins.then_inc(sems[inc], inck)
        self.q[eng].append(run)

    def w(self, *names):
        return [(n, self.cnt[n]) for n in names]

    def barrier(self, mark_fns, extra_dma=()):
        marks = []
        pre = [(s, self.cnt[s]) for s in ("dve", "act", "pool")]
        for e, s in (("tensor", "pe"), ("vector", "dve"), ("scalar", "act"),
                     ("gpsimd", "pool")):
            self.op(e, mark_fns[e], waits=(pre if e == "tensor" else ()),
                    inc=s)
            marks.append((s, self.cnt[s]))
        waits = marks + [(n, self.cnt[n]) for n in extra_dma]
        for e in ENGS:
            self.op(e, mark_fns[e] if e in mark_fns else (lambda Eg: Eg.nop()),
                    waits=waits)


def build_nc():
    nc = bass.Bass("TRN2", target_bir_lowering=False)
    em = Em()

    x_h = nc.dram_tensor("x", [NTOK, D], F32, kind="ExternalInput").ap()
    wq_h = nc.dram_tensor("wqT", [D, D], F32, kind="ExternalInput").ap()
    wk_h = nc.dram_tensor("wkT", [D, D], F32, kind="ExternalInput").ap()
    wv_h = nc.dram_tensor("wvT", [D, D], F32, kind="ExternalInput").ap()
    wo_h = nc.dram_tensor("woT", [D, D], F32, kind="ExternalInput").ap()
    qgam_h = nc.dram_tensor("qgam", [128, D], F32, kind="ExternalInput").ap()
    qbet_h = nc.dram_tensor("qbet", [128, D], F32, kind="ExternalInput").ap()
    kgam_h = nc.dram_tensor("kgam", [128, D], F32, kind="ExternalInput").ap()
    kbet_h = nc.dram_tensor("kbet", [128, D], F32, kind="ExternalInput").ap()
    wr_h = nc.dram_tensor("wrT", [128, 8 * NE], F32, kind="ExternalInput").ap()
    w1_h = nc.dram_tensor("w1T", [NE, D, D], F32, kind="ExternalInput").ap()
    w2_h = nc.dram_tensor("w2T", [NE, D, D], F32, kind="ExternalInput").ap()
    ws1_h = nc.dram_tensor("ws1T", [D, FS], F32, kind="ExternalInput").ap()
    ws2_h = nc.dram_tensor("ws2T", [FS, D], F32, kind="ExternalInput").ap()
    out_h = nc.dram_tensor("out", [NTOK, D], F32, kind="ExternalOutput").ap()
    y_h = nc.dram_tensor("y_scr", [NTOK, D], F32, kind="Internal").ap()

    x_r = x_h.rearrange("(n p) c -> p n c", p=128)
    y_r = y_h.rearrange("(n p) c -> p n c", p=128)
    o_r = out_h.rearrange("(n p) c -> p n c", p=128)

    ctx = ExitStack()
    sems = {}
    for s in ("pe", "dve", "act", "pool", "sp",
              "dx", "dy", "dyl", "dw", "do"):
        sems[s] = ctx.enter_context(nc.semaphore(s))

    c_iota = ctx.enter_context(nc.sbuf_tensor("c_iota", [128, 128], F32))
    c_id = ctx.enter_context(nc.sbuf_tensor("c_id", [128, 128], BF16))
    c_mask = ctx.enter_context(nc.sbuf_tensor("c_mask", [128, 128], BF16))
    c_eps = ctx.enter_context(nc.sbuf_tensor("c_eps", [128, 1], F32))
    c_sq = ctx.enter_context(nc.sbuf_tensor("c_sq", [128, D], F32))
    c_t1 = ctx.enter_context(nc.sbuf_tensor("c_t1", [128, D], F32))
    c_ln = ctx.enter_context(nc.sbuf_tensor("c_ln", [128, D], F32))
    c_st = ctx.enter_context(nc.sbuf_tensor("c_st", [128, 128], F32))
    c_wr = ctx.enter_context(nc.sbuf_tensor("c_wr", [128, 8 * NE], BF16))
    p_all = ctx.enter_context(nc.psum_tensor("p_all", [128, 3072], F32))
    p_bf = ctx.enter_context(nc.psum_tensor("p_bf", [128, 2048], BF16))


    em.op("gpsimd", lambda Eg: Eg.iota(c_iota[:], pattern=[[1, 128]], base=0,
                                       channel_multiplier=-1,
                                       allow_small_or_imprecise_dtypes=True),
          inc="pool")
    em.op("vector", lambda Eg: Eg.tensor_scalar(
        c_id[:], c_iota[:], 0.0, None, ALU.is_equal), waits=em.w("pool"))
    em.op("vector", lambda Eg: Eg.tensor_scalar(
        c_mask[:], c_iota[:], 0.0, None, ALU.is_ge))
    em.op("vector", lambda Eg: Eg.memset(c_eps[:], EPS), inc="dve")
    em.op("gpsimd", lambda Eg: Eg.dma_start(
        out=c_wr[:], in_=wr_h), inc="dw", inck=16)

    actx = ExitStack()
    s_x = [actx.enter_context(nc.sbuf_tensor(f"s_x{i}", [128, 2, D], F32))
           for i in range(2)]
    s_y = [actx.enter_context(nc.sbuf_tensor(f"s_y{i}", [128, 2, D], F32))
           for i in range(2)]
    s_wq = actx.enter_context(nc.sbuf_tensor("s_wq", [128, 8, D], BF16))
    s_wk = actx.enter_context(nc.sbuf_tensor("s_wk", [128, 8, D], BF16))
    s_wv = actx.enter_context(nc.sbuf_tensor("s_wv", [128, 8, D], BF16))
    s_wo = actx.enter_context(nc.sbuf_tensor("s_wo", [128, 8, D], BF16))
    s_qgam = actx.enter_context(nc.sbuf_tensor("s_qgam", [128, D], BF16))
    s_qbet = actx.enter_context(nc.sbuf_tensor("s_qbet", [128, D], BF16))
    s_kgam = actx.enter_context(nc.sbuf_tensor("s_kgam", [128, D], BF16))
    s_kbet = actx.enter_context(nc.sbuf_tensor("s_kbet", [128, D], BF16))
    s_h1 = [actx.enter_context(nc.sbuf_tensor(f"s_h1{i}", [128, D], BF16))
            for i in range(2)]
    c_sq2 = actx.enter_context(nc.sbuf_tensor("c_sq2", [128, D], F32))
    s_h1T = actx.enter_context(nc.sbuf_tensor("s_h1T", [128, D], BF16))
    s_t2 = actx.enter_context(nc.sbuf_tensor("s_t2", [128, D], BF16))
    s_t3 = actx.enter_context(nc.sbuf_tensor("s_t3", [128, D], BF16))
    s_t2b = actx.enter_context(nc.sbuf_tensor("s_t2b", [128, D], BF16))
    s_t3b = actx.enter_context(nc.sbuf_tensor("s_t3b", [128, D], BF16))
    s_qh = actx.enter_context(nc.sbuf_tensor("s_qh", [128, D], BF16))
    s_kh = actx.enter_context(nc.sbuf_tensor("s_kh", [128, D], BF16))
    s_qT = actx.enter_context(nc.sbuf_tensor("s_qT", [64, 2 * D], BF16))
    s_kT = actx.enter_context(nc.sbuf_tensor("s_kT", [64, 2 * D], BF16))
    s_v = actx.enter_context(nc.sbuf_tensor("s_v", [128, D], BF16))
    s_at = actx.enter_context(nc.sbuf_tensor("s_at", [128, H * 128], BF16))
    s_o = actx.enter_context(nc.sbuf_tensor("s_o", [128, D], BF16))
    s_oT = actx.enter_context(nc.sbuf_tensor("s_oT", [128, D], BF16))
    s_S = actx.enter_context(nc.sbuf_tensor("s_S", [64, D], F32))
    s_Sb = actx.enter_context(nc.sbuf_tensor("s_Sb", [64, D], BF16))

    for w_hb, s_w in ((wq_h, s_wq), (wk_h, s_wk), (wv_h, s_wv), (wo_h, s_wo)):
        em.op("gpsimd", lambda Eg, w_hb=w_hb, s_w=s_w: Eg.dma_start(
            out=s_w[:], in_=w_hb.rearrange("(i p) f -> p i f", p=128)),
            inc="dw", inck=16)
    DW_QK = em.cnt["dw"] - 32
    DW_V = em.cnt["dw"] - 16
    DW_O = em.cnt["dw"]
    for g_hb, s_g0 in ((qgam_h, s_qgam), (qbet_h, s_qbet),
                       (kgam_h, s_kgam), (kbet_h, s_kbet)):
        em.op("gpsimd", lambda Eg, g_hb=g_hb, s_g0=s_g0: Eg.dma_start(
            out=s_g0[:], in_=g_hb), inc="dw", inck=16)
    DW_ATTN = em.cnt["dw"]

    def ln_smalls(n, b=0):
        cs = lambda j: c_st[:, b + j:b + j + 1]
        em.op("gpsimd", lambda Eg: Eg.tensor_scalar(
            cs(2), cs(0), -1.0 / n, None, ALU.mult),
            waits=em.w("act", "dve"))
        em.op("gpsimd", lambda Eg: Eg.tensor_tensor(
            cs(6), cs(2), cs(2), ALU.mult))
        em.op("gpsimd", lambda Eg: Eg.tensor_scalar(
            cs(3), cs(1), 1.0 / n, None, ALU.mult))
        em.op("gpsimd", lambda Eg: Eg.tensor_tensor(
            cs(3), cs(3), cs(6), ALU.subtract), inc="pool")
        em.op("scalar", lambda Eg: Eg.activation(
            cs(7), cs(3), AF.Ln, bias=c_eps[:]), waits=em.w("pool"))
        em.op("scalar", lambda Eg: Eg.activation(
            cs(4), cs(7), AF.Exp, scale=-0.5), inc="act")
        em.op("gpsimd", lambda Eg: Eg.tensor_tensor(
            cs(5), cs(2), cs(4), ALU.mult), waits=em.w("act"), inc="pool")

    st_y = 0
    tile_marks = {}
    n_tiles = NCHUNK // 2
    h1_mark = {}
    marks = {}

    def emit_xln(cc):
        tl2 = cc // 2
        xcc = s_x[tl2 % 2][:, cc % 2, :]
        W_X2 = [("dx", (tl2 + 1) * 16)]
        em.op("vector", lambda Eg, xcc=xcc: Eg.tensor_reduce(
            c_st[:, 104:105], xcc, AX.X, ALU.add),
            waits=W_X2 + em.w("act", "pool"))
        em.op("scalar", lambda Eg, xcc=xcc: Eg.activation(
            c_sq2[:], xcc, AF.Square, accum_out=c_st[:, 105:106]),
            waits=W_X2 + em.w("dve", "pool"), inc="act")
        ln_smalls(D, b=104)
        em.op("vector", lambda Eg, xcc=xcc, cc=cc: Eg.tensor_scalar(
            s_h1[cc % 2][:], xcc, c_st[:, 108:109], c_st[:, 109:110],
            ALU.mult, ALU.add),
            waits=em.w("pe", "act", "pool"), inc="dve")
        h1_mark[cc] = em.cnt["dve"]

    for c in range(NCHUNK):
        tl, m = c // 2, c % 2
        l = c % (T // CH)
        par = tl % 2
        xc = s_x[par][:, m, :]

        if m == 0:
            if tl == 0:
                em.op("sync", lambda Eg: Eg.dma_start(
                    out=s_x[0][:], in_=x_r[:, 0:2, :]),
                    waits=em.w("dve", "pool"), inc="dx", inck=16)
                if n_tiles > 1:
                    em.op("sync", lambda Eg: Eg.dma_start(
                        out=s_x[1][:], in_=x_r[:, 2:4, :]),
                        inc="dx", inck=16)
            elif tl + 1 < n_tiles:
                em.op("sync", lambda Eg, tl=tl: Eg.dma_start(
                    out=s_x[(tl + 1) % 2][:],
                    in_=x_r[:, (tl + 1) * 2:(tl + 1) * 2 + 2, :]),
                    waits=tile_marks.get(tl - 1, ()), inc="dx", inck=16)

        if c == 0:
            emit_xln(0)

        tw = [("dve", h1_mark[c]), ("act", marks.get(("otev", c - 1), 0))]
        for j in range(8):
            em.op("tensor", lambda Eg, j=j, c=c: Eg.transpose(
                p_bf[:, j * 128:(j + 1) * 128],
                s_h1[c % 2][:, j * 128:(j + 1) * 128],
                c_id[:]), waits=(tw if j == 0 else ()),
                inc=("pe" if j == 7 else None), tag="tp")
        em.op("vector", lambda Eg: Eg.tensor_copy(s_h1T[:], p_bf[:, 0:D]),
              waits=em.w("pe"), inc="dve", tag="tp")
        marks[("h1T", c)] = em.cnt["dve"]

        h1T = s_h1T[:].rearrange("p (i t) -> p i t", i=8)
        qk_w = [("dw", DW_QK),
                ("dve", max(marks[("h1T", c)],
                            marks.get(("atev", c - 1), 0),
                            marks.get(("sadd", c - 1), 0))),
                ("act", marks.get(("qkT", c - 1), 0))]
        for i in range(8):
            for s_w, base in ((s_wq, 0), (s_wk, 1024)):
                for hf in range(2):
                    em.op("tensor",
                          lambda Eg, i=i, s_w=s_w, base=base, hf=hf: Eg.matmul(
                              p_all[:, base + hf * 512: base + (hf + 1) * 512],
                              h1T[:, i, :], s_w[:, i, hf * 512:(hf + 1) * 512],
                              start=(i == 0), stop=(i == 7)),
                          waits=(qk_w if i == 0 else ()),
                          inc="pe" if (i == 7 and base == 1024 and hf == 1)
                          else None, tag="qkv")
        QK_PE = em.cnt["pe"]
        v_w = [("dw", DW_V),
               ("dve", marks.get(("pln", c - 1), 0)),
               ("act", marks.get(("psq", c - 1), 0))]
        for i in range(8):
            for hf in range(2):
                em.op("tensor", lambda Eg, i=i, hf=hf: Eg.matmul(
                    p_all[:, 2048 + hf * 512: 2048 + (hf + 1) * 512],
                    h1T[:, i, :], s_wv[:, i, hf * 512:(hf + 1) * 512],
                    start=(i == 0), stop=(i == 7)),
                    waits=(v_w if i == 0 else ()),
                    inc="pe" if (i == 7 and hf == 1) else None, tag="qkv")
        QKV_PE = em.cnt["pe"]

        em.op("scalar", lambda Eg: Eg.copy(s_v[:], p_all[:, 2048:3072]),
              waits=[("pe", QKV_PE)], inc="act")

        if c + 1 < NCHUNK:
            emit_xln(c + 1)

        em.op("vector", lambda Eg: Eg.tensor_reduce(
            c_st[:, 8:24].unsqueeze(2),
            p_all[:, 0:D].rearrange("p (h d) -> p h d", d=DH),
            AX.X, ALU.add),
            waits=[("pe", QK_PE)] + em.w("act"), inc="dve")
        em.op("scalar", lambda Eg: Eg.activation(
            c_sq[:], p_all[:, 0:D], AF.Square),
            waits=[("pe", QK_PE)] + em.w("dve"), inc="act")
        em.op("vector", lambda Eg: Eg.tensor_reduce(
            c_st[:, 56:72].unsqueeze(2),
            p_all[:, 1024:2048].rearrange("p (h d) -> p h d", d=DH),
            AX.X, ALU.add), waits=em.w("act"))
        em.op("vector", lambda Eg: Eg.tensor_reduce(
            c_st[:, 24:40].unsqueeze(2),
            c_sq[:].rearrange("p (h d) -> p h d", d=DH), AX.X, ALU.add),
            inc="dve")
        em.op("scalar", lambda Eg: Eg.activation(
            c_sq[:], p_all[:, 1024:2048], AF.Square),
            waits=em.w("dve"), inc="act")
        marks[("ksq", c)] = em.cnt["act"]
        em.op("vector", lambda Eg: Eg.tensor_reduce(
            c_st[:, 72:88].unsqueeze(2),
            c_sq[:].rearrange("p (h d) -> p h d", d=DH), AX.X, ALU.add),
            waits=em.w("act"), inc="dve")
        em.op("gpsimd", lambda Eg: Eg.tensor_scalar(
            c_st[:, 8:24], c_st[:, 8:24], -1.0 / DH, None, ALU.mult),
            waits=em.w("dve"))
        em.op("gpsimd", lambda Eg: Eg.tensor_scalar(
            c_st[:, 56:72], c_st[:, 56:72], -1.0 / DH, None, ALU.mult))
        em.op("gpsimd", lambda Eg: Eg.tensor_tensor(
            c_st[:, 40:56], c_st[:, 8:24], c_st[:, 8:24], ALU.mult))
        em.op("gpsimd", lambda Eg: Eg.tensor_tensor(
            c_st[:, 88:104], c_st[:, 56:72], c_st[:, 56:72], ALU.mult))
        em.op("gpsimd", lambda Eg: Eg.tensor_scalar(
            c_st[:, 24:40], c_st[:, 24:40], 1.0 / DH, None, ALU.mult))
        em.op("gpsimd", lambda Eg: Eg.tensor_scalar(
            c_st[:, 72:88], c_st[:, 72:88], 1.0 / DH, None, ALU.mult))
        em.op("gpsimd", lambda Eg: Eg.tensor_tensor(
            c_st[:, 24:40], c_st[:, 24:40], c_st[:, 40:56], ALU.subtract))
        em.op("gpsimd", lambda Eg: Eg.tensor_tensor(
            c_st[:, 72:88], c_st[:, 72:88], c_st[:, 88:104], ALU.subtract),
            inc="pool")
        em.op("scalar", lambda Eg: Eg.activation(
            c_st[:, 24:40], c_st[:, 24:40], AF.Ln, bias=c_eps[:]),
            waits=em.w("pool"))
        em.op("scalar", lambda Eg: Eg.activation(
            c_st[:, 24:40], c_st[:, 24:40], AF.Exp, scale=-0.5))
        em.op("scalar", lambda Eg: Eg.activation(
            c_st[:, 72:88], c_st[:, 72:88], AF.Ln, bias=c_eps[:]))
        em.op("scalar", lambda Eg: Eg.activation(
            c_st[:, 72:88], c_st[:, 72:88], AF.Exp, scale=-0.5), inc="act")
        em.op("vector", lambda Eg: Eg.tensor_tensor(
            c_t1[:].rearrange("p (h d) -> p h d", d=DH),
            p_all[:, 0:D].rearrange("p (h d) -> p h d", d=DH),
            c_st[:, 8:24].to_broadcast([128, H, DH]), ALU.add),
            waits=em.w("pool"))
        em.op("vector", lambda Eg: Eg.tensor_tensor(
            c_ln[:].rearrange("p (h d) -> p h d", d=DH),
            p_all[:, 1024:2048].rearrange("p (h d) -> p h d", d=DH),
            c_st[:, 56:72].to_broadcast([128, H, DH]), ALU.add))
        em.op("vector", lambda Eg: Eg.tensor_tensor(
            s_t2[:].rearrange("p (h d) -> p h d", d=DH),
            c_t1[:].rearrange("p (h d) -> p h d", d=DH),
            c_st[:, 24:40].to_broadcast([128, H, DH]), ALU.mult),
            waits=em.w("act", "pool"))
        em.op("vector", lambda Eg: Eg.tensor_tensor(
            s_t2b[:].rearrange("p (h d) -> p h d", d=DH),
            c_ln[:].rearrange("p (h d) -> p h d", d=DH),
            c_st[:, 72:88].to_broadcast([128, H, DH]), ALU.mult),
            inc="dve")
        em.op("gpsimd", lambda Eg: Eg.tensor_tensor(
            s_t3b[:], s_t2b[:], s_kgam[:], ALU.mult),
            waits=[("dw", DW_ATTN)] + em.w("dve"))
        em.op("vector", lambda Eg: Eg.tensor_tensor(
            s_t3[:], s_t2[:], s_qgam[:], ALU.mult),
            waits=[("dw", DW_ATTN)])
        em.op("vector", lambda Eg: Eg.tensor_tensor(
            s_qh[:], s_t3[:], s_qbet[:], ALU.add), inc="dve")
        QH_DVE = em.cnt["dve"]
        em.op("gpsimd", lambda Eg: Eg.tensor_tensor(
            s_kh[:], s_t3b[:], s_kbet[:], ALU.add), inc="pool")
        KH_POOL = em.cnt["pool"]

        for (hsrc, dstT, tw2) in ((s_qh, s_qT, None),
                                  (s_kh, s_kT, None)):
            tw2 = ([("dve", QH_DVE)] if hsrc is s_qh
                   else [("pool", KH_POOL), ("act", em.cnt["act"])])
            for h in range(H):
                em.op("tensor", lambda Eg, hsrc=hsrc, h=h: Eg.matmul(
                    p_all[0:64, h * 128:(h + 1) * 128],
                    hsrc[:, h * 64:(h + 1) * 64], c_id[:],
                    start=True, stop=True),
                    waits=(tw2 if h == 0 else ()),
                    inc="pe" if h == H - 1 else None, tag="tp")
            em.op("scalar", lambda Eg, dstT=dstT: Eg.copy(
                dstT[:], p_all[0:64, 0:2048]),
                waits=em.w("pe"), inc="act", tag="tp")
        QKT_ACT = em.cnt["act"]
        marks[("qkT", c)] = QKT_ACT

        for h in range(H):
            em.op("tensor", lambda Eg, h=h: Eg.matmul(
                p_all[:, h * 128:(h + 1) * 128],
                s_kT[0:64, h * 128:(h + 1) * 128],
                s_qT[0:64, h * 128:(h + 1) * 128],
                start=True, stop=True),
                waits=([("act", QKT_ACT)] if h == 0 else ()),
                inc="pe" if h == H - 1 else None, tag="at")
        AT_PE = em.cnt["pe"]
        em.op("vector", lambda Eg: Eg.tensor_tensor(
            s_at[:].rearrange("p (h t) -> p h t", h=H),
            p_all[:, 0:2048].rearrange("p (h t) -> p h t", h=H),
            c_mask[:].unsqueeze(1).to_broadcast([128, H, 128]), ALU.mult),
            waits=[("pe", AT_PE)], inc="dve", tag="red3")
        AT_DVE = em.cnt["dve"]
        marks[("atev", c)] = AT_DVE

        for h in range(H):
            last_o = (h == H - 1) and l == 0
            em.op("tensor", lambda Eg, h=h, l=l: Eg.matmul(
                p_all[:, 2048 + h * 64: 2048 + (h + 1) * 64],
                s_at[:, h * 128:(h + 1) * 128], s_v[:, h * 64:(h + 1) * 64],
                start=True, stop=(l == 0)),
                waits=(([("dve", AT_DVE)] + em.w("act")) if h == 0 else ()),
                inc="pe" if last_o else None, tag="at")
            if l > 0:
                em.op("tensor", lambda Eg, h=h: Eg.matmul(
                    p_all[:, 2048 + h * 64: 2048 + (h + 1) * 64],
                    s_qT[0:64, h * 128:(h + 1) * 128],
                    s_Sb[0:64, h * 64:(h + 1) * 64],
                    start=False, stop=True),
                    waits=(em.w("pool") if h == 0 else ()),
                    inc="pe" if h == H - 1 else None, tag="sup")
        O_PE = em.cnt["pe"]

        for h in range(H):
            em.op("tensor", lambda Eg, h=h: Eg.matmul(
                p_all[0:64, h * 64:(h + 1) * 64],
                s_kh[:, h * 64:(h + 1) * 64], s_v[:, h * 64:(h + 1) * 64],
                start=True, stop=True),
                waits=(([("dve", AT_DVE)] + em.w("act")) if h == 0 else ()),
                inc="pe" if h == H - 1 else None, tag="sup")
        SD_PE = em.cnt["pe"]
        if l == 0:
            em.op("vector", lambda Eg: Eg.tensor_copy(
                s_S[:], p_all[0:64, 0:D]),
                waits=[("pe", SD_PE), ("pe", O_PE)], inc="dve", tag="sup")
        else:
            em.op("vector", lambda Eg: Eg.tensor_tensor(
                s_S[:], s_S[:], p_all[0:64, 0:D], ALU.add),
                waits=[("pe", SD_PE), ("pe", O_PE)], inc="dve", tag="sup")
        marks[("sadd", c)] = em.cnt["dve"]
        em.op("gpsimd", lambda Eg: Eg.tensor_copy(s_Sb[:], s_S[:]),
              waits=em.w("dve"), inc="pool", tag="sup")

        em.op("scalar", lambda Eg: Eg.copy(s_o[:], p_all[:, 2048:3072]),
              waits=[("pe", O_PE)], inc="act")
        for j in range(8):
            em.op("tensor", lambda Eg, j=j: Eg.transpose(
                p_bf[:, j * 128:(j + 1) * 128], s_o[:, j * 128:(j + 1) * 128],
                c_id[:]), waits=(em.w("act", "dve") if j == 0 else ()),
                inc="pe" if j == 7 else None, tag="tp")
        em.op("scalar", lambda Eg: Eg.copy(s_oT[:], p_bf[:, 0:D]),
              waits=em.w("pe"), inc="act", tag="tp")
        OT_ACT = em.cnt["act"]
        marks[("otev", c)] = OT_ACT

        oT = s_oT[:].rearrange("p (i t) -> p i t", i=8)
        for i in range(8):
            for hf in range(2):
                em.op("tensor", lambda Eg, i=i, hf=hf: Eg.matmul(
                    p_all[:, 2048 + hf * 512: 2048 + (hf + 1) * 512],
                    oT[:, i, :], s_wo[:, i, hf * 512:(hf + 1) * 512],
                    start=(i == 0), stop=(i == 7)),
                    waits=(([("dw", DW_O), ("act", OT_ACT)] + em.w("dve"))
                           if i == 0 else ()),
                    inc="pe" if (i == 7 and hf == 1) else None, tag="ao")
        AO_PE = em.cnt["pe"]

        em.op("vector", lambda Eg: Eg.tensor_reduce(
            c_st[:, 0:1], p_all[:, 2048:3072], AX.X, ALU.add),
            waits=[("pe", AO_PE)] + em.w("act"), inc="dve", tag="ao")
        em.op("scalar", lambda Eg: Eg.activation(
            c_sq[:], p_all[:, 2048:3072], AF.Square, accum_out=c_st[:, 1:2]),
            waits=[("pe", AO_PE)] + em.w("dve"), inc="act", tag="actp")
        marks[("psq", c)] = em.cnt["act"]
        ln_smalls(D)
        em.op("vector", lambda Eg: Eg.tensor_scalar(
            c_ln[:], p_all[:, 2048:3072], c_st[:, 4:5], c_st[:, 5:6],
            ALU.mult, ALU.add), waits=em.w("act", "pool"),
            inc="dve", tag="ao")
        marks[("pln", c)] = em.cnt["dve"]
        em.op("gpsimd", lambda Eg, xc=xc, par=par, m=m: Eg.tensor_tensor(
            s_y[par][:, m, :], xc, c_ln[:], ALU.add),
            waits=em.w("dve") + [("dy", st_y * 16)], inc="pool")

        if DBG and c == 0:
            dumps = [(s_h1, 4), (s_qh, 6), (s_kh, 8), (s_v, 10),
                     (s_o, 12), (s_oT, 14), (s_h1T, 16), (c_st, 18),
                     (c_ln, 20), (c_sq, 22)]
            for ten, rb in dumps:
                em.op("gpsimd", lambda Eg, ten=ten, rb=rb: Eg.dma_start(
                    out=o_r[:, rb, 0:ten.shape[-1]], in_=ten[:]
                    if len(ten.shape) == 2 else ten[:].flatten_outer_dims()),
                    waits=em.w("dve", "act", "pool", "pe"), inc="do", inck=16)
            em.op("gpsimd", lambda Eg: Eg.dma_start(
                out=o_r[0:64, 24:26, :],
                in_=s_qT[:].rearrange("p (a b) -> p a b", a=2)),
                waits=em.w("dve", "act", "pool", "pe"), inc="do", inck=16)
            em.op("gpsimd", lambda Eg: Eg.dma_start(
                out=o_r[0:64, 26:28, :],
                in_=s_kT[:].rearrange("p (a b) -> p a b", a=2)),
                inc="do", inck=16)
            em.op("gpsimd", lambda Eg: Eg.dma_start(
                out=o_r[:, 28:30, :],
                in_=s_at[:].rearrange("p (a b) -> p a b", a=2)),
                inc="do", inck=16)
        if m == 1:
            em.op("sync", lambda Eg, tl=tl, par=par: Eg.dma_start(
                out=y_r[:, tl * 2:tl * 2 + 2, :], in_=s_y[par][:]),
                waits=em.w("pool"), inc="dy", inck=16)
            st_y += 1
            tile_marks[tl] = em.w("dve", "act", "pool")
            if PHASE == "attn_only":
                em.op("sync", lambda Eg, tl=tl, par=par: Eg.dma_start(
                    out=o_r[:, tl * 2:tl * 2 + 2, :], in_=s_y[par][:]),
                    inc="do", inck=16)

    DY_ALL = em.cnt["dy"]

    if PHASE == "full":
        mark_fns = {
            "tensor": lambda Eg: Eg.matmul(
                p_all[:, 0:128], c_id[:], c_id[:], start=True, stop=True),
            "vector": lambda Eg: Eg.tensor_copy(c_st[:, 126:127], c_eps[:]),
            "scalar": lambda Eg: Eg.copy(c_st[:, 125:126], c_eps[:]),
            "gpsimd": lambda Eg: Eg.tensor_copy(c_st[:, 124:125], c_eps[:]),
        }
        em.barrier(mark_fns, extra_dma=("dx", "dy"))
        actx.close()

        fctx = ExitStack()
        s_sl = [fctx.enter_context(
            nc.sbuf_tensor(f"s_sl{i}", [128, 8, D], BF16)) for i in range(4)]
        s_h2T = fctx.enter_context(
            nc.sbuf_tensor("s_h2T", [128, 8, 2048], BF16))
        s_acc = fctx.enter_context(nc.sbuf_tensor("s_acc", [128, 16, D], BF16))
        s_is = fctx.enter_context(nc.sbuf_tensor("s_is", [128, 16, 512], BF16))
        s_fy = fctx.enter_context(nc.sbuf_tensor("s_fy", [128, 4, D], F32))
        s_h2 = fctx.enter_context(nc.sbuf_tensor("s_h2", [128, 4, D], BF16))
        s_rl = fctx.enter_context(nc.sbuf_tensor("s_rl", [128, 2048], BF16))
        s_tg = fctx.enter_context(nc.sbuf_tensor("s_tg", [128, 2, D], BF16))
        s_fo = fctx.enter_context(nc.sbuf_tensor("s_fo", [128, 4, D], F32))
        s_lg = fctx.enter_context(nc.sbuf_tensor("s_lg", [128, 16, NE], F32))
        s_g = fctx.enter_context(nc.sbuf_tensor("s_g", [128, 16, NE], F32))
        s_gs = fctx.enter_context(nc.sbuf_tensor("s_gs", [128, 96], F32))

        NGRP = 2
        wld = [0]

        for g in range(NGRP):
            unit_srcs = [
                ws1_h[:, 0:1024].rearrange("(i p) f -> p i f", p=128),
                ws1_h[:, 1024:2048].rearrange("(i p) f -> p i f", p=128),
                ws2_h[0:1024, :].rearrange("(i p) f -> p i f", p=128),
                ws2_h[1024:2048, :].rearrange("(i p) f -> p i f", p=128),
            ]
            for e in range(NE):
                unit_srcs.append(w1_h[e].rearrange("(i p) f -> p i f", p=128))
                unit_srcs.append(w2_h[e].rearrange("(i p) f -> p i f", p=128))
            unit_wait = {}
            loaded = [-1]
            ev_mark = {}
            ev_mark2 = {}

            def issue_unit(u, unit_srcs=unit_srcs, unit_wait=unit_wait,
                           loaded=loaded):
                slot = s_sl[u % 4]
                wld[0] += 1
                em.op("gpsimd",
                      lambda Eg, slot=slot, src=unit_srcs[u]: Eg.dma_start(
                          out=slot[:], in_=src),
                      waits=em.w("pe"), inc="dw", inck=16)
                unit_wait[u] = em.cnt["dw"]
                loaded[0] = u

            for u in range(4):
                issue_unit(u)

            for tt in range(4):
                gt0 = g * 16 + tt * 4
                em.op("sync", lambda Eg, gt0=gt0: Eg.dma_start(
                    out=s_fy[:], in_=y_r[:, gt0:gt0 + 4, :]),
                    waits=[("dy", DY_ALL)] + em.w("dve", "act", "pool"),
                    inc="dyl", inck=16)
                DYL = em.cnt["dyl"]
                for n in range(4):
                    em.op("vector", lambda Eg, n=n: Eg.tensor_reduce(
                        c_st[:, 0:1], s_fy[:, n, :], AX.X, ALU.add),
                        waits=[("dyl", DYL)] + em.w("act"))
                    em.op("scalar", lambda Eg, n=n: Eg.activation(
                        c_sq[:], s_fy[:, n, :], AF.Square,
                        accum_out=c_st[:, 1:2]),
                        waits=[("dyl", DYL)] + em.w("dve"), inc="act")
                    ln_smalls(D)
                    em.op("vector", lambda Eg, n=n: Eg.tensor_scalar(
                        s_h2[:, n, :], s_fy[:, n, :], c_st[:, 4:5],
                        c_st[:, 5:6], ALU.mult, ALU.add),
                        waits=em.w("pe", "act", "pool"), inc="dve")
                    for j in range(8):
                        em.op("tensor", lambda Eg, n=n, j=j: Eg.transpose(
                            p_bf[:, j * 128:(j + 1) * 128],
                            s_h2[:, n, j * 128:(j + 1) * 128], c_id[:]),
                            waits=(em.w("dve") if j == 0 else ()),
                            inc="pe" if j == 7 else None)
                    tc0 = (tt * 4 + n) * 128
                    em.op("vector", lambda Eg, tc0=tc0: Eg.tensor_copy(
                        s_h2T[:, :, tc0:tc0 + 128],
                        p_bf[:, 0:D].rearrange("p (i t) -> p i t", i=8)),
                        waits=em.w("pe"), inc="dve")
                    H2T_DVE = em.cnt["dve"]
                    for i in range(8):
                        em.op("tensor", lambda Eg, tc0=tc0, i=i: Eg.matmul(
                            p_all[:, 0:NE], s_h2T[:, i, tc0:tc0 + 128],
                            c_wr[:, i * NE:(i + 1) * NE],
                            start=(i == 0), stop=(i == 7)),
                            waits=([("dve", H2T_DVE), ("dw", 16)]
                                   if i == 0 else ()),
                            inc="pe" if i == 7 else None)
                    em.op("vector", lambda Eg, tt=tt, n=n: Eg.tensor_copy(
                        s_lg[:, tt * 4 + n, :], p_all[:, 0:NE]),
                        waits=em.w("pe"), inc="dve")

            m1 = s_gs[:, 0:16]
            m2 = s_gs[:, 16:32]
            w1c = s_gs[:, 32:48]
            w2c = s_gs[:, 48:64]
            ee = s_gs[:, 64:80]
            m1b = s_gs[:, 80:96]
            lg3 = s_lg[:]
            mk3 = c_sq[:, 0:128].rearrange("p (a b) -> p a b", a=16)
            e23 = c_ln[:, 0:128].rearrange("p (a b) -> p a b", a=16)
            em.op("vector", lambda Eg: Eg.tensor_reduce(
                m1.unsqueeze(2), lg3, AX.X, ALU.max),
                waits=em.w("act", "pool"), inc="dve")
            em.op("gpsimd", lambda Eg: Eg.tensor_copy(m1b, m1),
                  waits=em.w("dve"), inc="pool")
            em.op("vector", lambda Eg: Eg.tensor_tensor(
                s_g[:], lg3, m1b.to_broadcast([128, 16, NE]), ALU.is_equal),
                waits=em.w("pool"), inc="dve")
            em.op("gpsimd", lambda Eg: Eg.tensor_scalar(
                mk3, s_g[:], -1e9, None, ALU.mult), waits=em.w("dve"))
            em.op("gpsimd", lambda Eg: Eg.tensor_tensor(
                mk3, mk3, lg3, ALU.add), inc="pool")
            em.op("vector", lambda Eg: Eg.tensor_reduce(
                m2.unsqueeze(2), mk3, AX.X, ALU.max),
                waits=em.w("pool"), inc="dve")
            em.op("gpsimd", lambda Eg: Eg.tensor_tensor(
                ee, m2, m1b, ALU.subtract), waits=em.w("dve"))
            em.op("gpsimd", lambda Eg: Eg.tensor_copy(m1b, m2),
                  inc="pool")
            em.op("scalar", lambda Eg: Eg.activation(
                ee, ee, AF.Exp), waits=em.w("pool"), inc="act")
            em.op("gpsimd", lambda Eg: Eg.tensor_scalar(
                w1c, ee, 1.0, None, ALU.add), waits=em.w("act"), inc="pool")
            em.op("vector", lambda Eg: Eg.reciprocal(w1c, w1c),
                  waits=em.w("pool"), inc="dve")
            em.op("vector", lambda Eg: Eg.tensor_tensor(
                e23, mk3, m1b.to_broadcast([128, 16, NE]), ALU.is_equal),
                inc="dve")
            em.op("gpsimd", lambda Eg: Eg.tensor_scalar(
                w2c, w1c, -1.0, None, ALU.mult), waits=em.w("dve"))
            em.op("gpsimd", lambda Eg: Eg.tensor_scalar(
                w2c, w2c, 1.0, None, ALU.add))
            em.op("gpsimd", lambda Eg: Eg.tensor_tensor(
                s_g[:], s_g[:], w1c.to_broadcast([128, 16, NE]), ALU.mult))
            em.op("gpsimd", lambda Eg: Eg.tensor_tensor(
                e23, e23, w2c.to_broadcast([128, 16, NE]), ALU.mult))
            em.op("gpsimd", lambda Eg: Eg.tensor_tensor(
                s_g[:], s_g[:], e23, ALU.add), inc="pool")

            def emit_F(tt, g=g):
                gt0 = g * 16 + tt * 4
                em.op("sync", lambda Eg, gt0=gt0: Eg.dma_start(
                    out=s_fy[:], in_=y_r[:, gt0:gt0 + 4, :]),
                    waits=em.w("dve", "act", "pool"), inc="dyl", inck=16)
                DYL = em.cnt["dyl"]
                for n in range(4):
                    em.op("vector", lambda Eg, tt=tt, n=n: Eg.tensor_reduce(
                        c_st[:, 0:1], s_acc[:, tt * 4 + n, :], AX.X, ALU.add),
                        waits=em.w("act", "pool"))
                    em.op("scalar", lambda Eg, tt=tt, n=n: Eg.activation(
                        c_sq[:], s_acc[:, tt * 4 + n, :], AF.Square,
                        accum_out=c_st[:, 1:2]),
                        waits=em.w("dve", "pool"), inc="act")
                    ln_smalls(D)
                    em.op("vector", lambda Eg, tt=tt, n=n: Eg.tensor_scalar(
                        c_ln[:], s_acc[:, tt * 4 + n, :], c_st[:, 4:5],
                        c_st[:, 5:6], ALU.mult, ALU.add),
                        waits=em.w("act", "pool"), inc="dve")
                    em.op("gpsimd", lambda Eg, n=n: Eg.tensor_tensor(
                        s_fo[:, n, :], s_fy[:, n, :], c_ln[:], ALU.add),
                        waits=em.w("dve") + [("dyl", DYL),
                                             ("do", em.cnt["do"])],
                        inc="pool")
                em.op("sync", lambda Eg, gt0=gt0: Eg.dma_start(
                    out=o_r[:, gt0:gt0 + 4, :], in_=s_fo[:]),
                    waits=em.w("pool"), inc="do", inck=16)

            for tt in range(4):
                ev1 = {}
                rl1 = {}
                for g8 in range(8):
                    u = g8 // 4
                    reg = (g8 % 2) * 1024
                    for jj in range(2):
                        for i in range(8):
                            em.op("tensor",
                                  lambda Eg, u=u, g8=g8, reg=reg, jj=jj, i=i,
                                  tt=tt: Eg.matmul(
                                      p_all[:, reg + jj * 512:
                                            reg + jj * 512 + 512],
                                      s_sl[u][:, i, (g8 % 4) * 256 + jj * 128:
                                              (g8 % 4) * 256 + jj * 128 + 128],
                                      s_h2T[:, i, tt * 512:(tt + 1) * 512],
                                      start=(i == 0), stop=(i == 7)),
                                  waits=([("dw", unit_wait[u]),
                                          ("dve", ev1.get(g8 % 2,
                                                          em.cnt["dve"]
                                                          if g8 < 2 else 0)),
                                          ("act", rl1.get(g8 % 2, 0))]
                                         if (jj == 0 and i == 0) else ()),
                                  inc="pe" if (jj == 1 and i == 7) else None)
                    PE_Q = em.cnt["pe"]
                    em.op("scalar", lambda Eg, reg=reg, g8=g8: Eg.activation(
                        s_rl[:, reg:reg + 1024], p_all[:, reg:reg + 1024],
                        AF.Relu), waits=[("pe", PE_Q)], inc="act")
                    rl1[g8 % 2] = em.cnt["act"]
                    em.op("vector", lambda Eg, g8=g8, reg=reg: Eg.tensor_tensor(
                        s_is[:, 2 * g8:2 * g8 + 2, :],
                        p_all[:, reg:reg + 1024].rearrange(
                            "p (a b) -> p a b", a=2),
                        s_rl[:, reg:reg + 1024].rearrange(
                            "p (a b) -> p a b", a=2), ALU.mult),
                        waits=em.w("act"), inc="dve")
                    ev1[g8 % 2] = em.cnt["dve"]
                IS_DVE = em.cnt["dve"]
                for n in range(4):
                    for hf in range(2):
                        for i in range(16):
                            u = 2 + i // 8
                            em.op("tensor",
                                  lambda Eg, i=i, u=u, hf=hf, n=n: Eg.matmul(
                                      p_all[:, 2048 + hf * 512:
                                            2048 + (hf + 1) * 512],
                                      s_is[:, i, n * 128:(n + 1) * 128],
                                      s_sl[u][:, i % 8,
                                              hf * 512:(hf + 1) * 512],
                                      start=(i == 0), stop=(i == 15)),
                                  waits=(([("dve", IS_DVE),
                                           ("dw", unit_wait[2]),
                                           ("dw", unit_wait[3]),
                                           ("dve", ev_mark.get(hf, 0))]
                                          ) if i == 0 else ()),
                                  inc="pe" if i == 15 else None)
                        em.op("vector",
                              lambda Eg, tt=tt, n=n, hf=hf: Eg.tensor_copy(
                                  s_acc[:, tt * 4 + n,
                                        hf * 512:(hf + 1) * 512],
                                  p_all[:, 2048 + hf * 512:
                                        2048 + (hf + 1) * 512]),
                              waits=em.w("pe"), inc="dve")
                        ev_mark[hf] = em.cnt["dve"]

            for e in range(NE):
                u1, u2 = 4 + 2 * e, 5 + 2 * e
                while loaded[0] < min(u2 + 2, len(unit_srcs) - 1):
                    issue_unit(loaded[0] + 1)
                sl1, sl2 = s_sl[u1 % 4], s_sl[u2 % 4]
                for tt in range(4):
                    ev1 = {}
                    rl1 = {}
                    for eg in range(4):
                        reg = (eg % 2) * 1024
                        for jj in range(2):
                            for i in range(8):
                                em.op("tensor",
                                      lambda Eg, sl1=sl1, eg=eg, reg=reg,
                                      jj=jj, i=i, tt=tt: Eg.matmul(
                                          p_all[:, reg + jj * 512:
                                                reg + jj * 512 + 512],
                                          sl1[:, i, eg * 256 + jj * 128:
                                              eg * 256 + jj * 128 + 128],
                                          s_h2T[:, i,
                                                tt * 512:(tt + 1) * 512],
                                          start=(i == 0), stop=(i == 7)),
                                      waits=([("dw", unit_wait[u1]),
                                              ("dve", ev1.get(eg % 2,
                                                              em.cnt["dve"]
                                                              if eg < 2
                                                              else 0)),
                                              ("act", rl1.get(eg % 2, 0))]
                                             if (jj == 0 and i == 0) else ()),
                                      inc="pe" if (jj == 1 and i == 7)
                                      else None)
                        PE_Q = em.cnt["pe"]
                        em.op("scalar", lambda Eg, reg=reg: Eg.activation(
                            s_rl[:, reg:reg + 1024], p_all[:, reg:reg + 1024],
                            AF.Relu), waits=[("pe", PE_Q)], inc="act")
                        rl1[eg % 2] = em.cnt["act"]
                        em.op("vector", lambda Eg, eg=eg, reg=reg:
                              Eg.tensor_tensor(
                                  s_is[:, 2 * eg:2 * eg + 2, :],
                                  p_all[:, reg:reg + 1024].rearrange(
                                      "p (a b) -> p a b", a=2),
                                  s_rl[:, reg:reg + 1024].rearrange(
                                      "p (a b) -> p a b", a=2), ALU.mult),
                              waits=em.w("act"), inc="dve")
                        ev1[eg % 2] = em.cnt["dve"]
                    IE_DVE = em.cnt["dve"]
                    for n in range(4):
                        for hf in range(2):
                            for i in range(8):
                                em.op("tensor",
                                      lambda Eg, sl2=sl2, i=i, hf=hf, n=n:
                                      Eg.matmul(
                                          p_all[:, 2048 + hf * 512:
                                                2048 + (hf + 1) * 512],
                                          s_is[:, i, n * 128:(n + 1) * 128],
                                          sl2[:, i, hf * 512:(hf + 1) * 512],
                                          start=(i == 0), stop=(i == 7)),
                                      waits=(([("dve", IE_DVE),
                                               ("dw", unit_wait[u2]),
                                               ("act", ev_mark2.get(hf, 0))]
                                              ) if i == 0 else ()),
                                      inc="pe" if i == 7 else None)
                            PE_O = em.cnt["pe"]
                            em.op("scalar", lambda Eg, tt=tt, n=n, e=e, hf=hf:
                                  Eg.activation(
                                      s_tg[:, n % 2, hf * 512:(hf + 1) * 512],
                                      p_all[:, 2048 + hf * 512:
                                            2048 + (hf + 1) * 512],
                                      AF.Copy,
                                      scale=s_g[:, tt * 4 + n, e:e + 1]),
                                  waits=[("pe", PE_O)] + em.w("dve", "pool"),
                                  inc="act")
                            ev_mark2[hf] = em.cnt["act"]
                        eng = "vector" if n % 2 == 0 else "gpsimd"
                        em.op(eng, lambda Eg, tt=tt, n=n: Eg.tensor_tensor(
                            s_acc[:, tt * 4 + n, :], s_acc[:, tt * 4 + n, :],
                            s_tg[:, n % 2, :], ALU.add),
                            waits=em.w("act"),
                            inc="dve" if eng == "vector" else "pool")
                    if e == NE - 1:
                        emit_F(tt)


    em.op("sync", lambda Eg: Eg.nop(), waits=[("do", em.cnt["do"])])

    blk = ctx.enter_context(nc.Block())
    for e in ENGS:
        def runner(Eobj, e=e):
            for fn in em.q[e]:
                fn(Eobj, sems)
        getattr(blk, e)(runner)
    if PHASE == "full":
        fctx.close()
    else:
        actx.close()
    ctx.close()
    return nc


_NC_CACHE = {}


def _get_nc():
    if PHASE not in _NC_CACHE:
        _NC_CACHE[PHASE] = build_nc()
    return _NC_CACHE[PHASE]


def kernel(L0_attn_x, L0_attn_Wq, L0_attn_Wk, L0_attn_Wv, L0_attn_Wo,
           L0_attn_q_gamma, L0_attn_q_beta, L0_attn_k_gamma, L0_attn_k_beta,
           L0_ffn_w_router, L0_ffn_w1_stack, L0_ffn_w2_stack,
           L0_ffn_ws1, L0_ffn_ws2):
    ca = np.ascontiguousarray
    x = ca(np.asarray(L0_attn_x, dtype=np.float32))
    wq = ca(np.asarray(L0_attn_Wq, np.float32).T)
    wk = ca(np.asarray(L0_attn_Wk, np.float32).T)
    wv = ca(np.asarray(L0_attn_Wv, np.float32).T)
    wo = ca(np.asarray(L0_attn_Wo, np.float32).T)
    qg = np.asarray(L0_attn_q_gamma, np.float32)
    qb = np.asarray(L0_attn_q_beta, np.float32)
    kg = np.asarray(L0_attn_k_gamma, np.float32)
    kb = np.asarray(L0_attn_k_beta, np.float32)
    rep = lambda v: ca(np.broadcast_to(np.tile(v, H)[None, :], (128, D)))
    wr = np.asarray(L0_ffn_w_router, np.float32).T
    wr = ca(wr.reshape(8, 128, NE).transpose(1, 0, 2).reshape(128, 8 * NE))
    w1 = ca(np.asarray(L0_ffn_w1_stack, np.float32).transpose(0, 2, 1))
    w2 = ca(np.asarray(L0_ffn_w2_stack, np.float32).transpose(0, 2, 1))
    ws1 = ca(np.asarray(L0_ffn_ws1, np.float32).T)
    ws2 = ca(np.asarray(L0_ffn_ws2, np.float32).T)

    shared = dict(wqT=wq, wkT=wk, wvT=wv, woT=wo,
                  qgam=rep(qg), qbet=rep(qb), kgam=rep(kg), kbet=rep(kb),
                  wrT=wr, w1T=w1, w2T=w2, ws1T=ws1, ws2T=ws2)
    in_maps = []
    for cid in range(N_CORES):
        m = dict(shared)
        m["x"] = ca(x[cid * NB:(cid + 1) * NB].reshape(NTOK, D))
        in_maps.append(m)

    nc = _get_nc()
    res = run_bass_kernel_spmd(nc, in_maps, core_ids=list(range(N_CORES)))
    outs = [np.asarray(r["out"], np.float32).reshape(NB, T, D)
            for r in res.results]
    return np.concatenate(outs, axis=0)
